# version 18
# baseline (speedup 1.0000x reference)
import numpy as np
import ml_dtypes
import concourse.bass as bass
import concourse.mybir as mybir
from concourse.bass_utils import run_bass_kernel_spmd

F32 = mybir.dt.float32
BF16 = mybir.dt.bfloat16
AF = mybir.ActivationFunctionType
ALU = mybir.AluOpType
AX = mybir.AxisListType

B, T, E = 4, 4096, 1024
H = 16
FF = 4096
MAGIC = float(2.0 ** 52)
SCALE96 = float(np.float32(1.0) / np.sqrt(np.float32(96.0), dtype=np.float32))
NEG_BIG = -1.0e30
LN_EPS = 1e-5
ENGINES = ("sync", "tensor", "vector", "scalar", "gpsimd")
KMU = [256 * (oi + 1) for oi in range(16)]


def host_tables(half):
    tiles = [2 * oi + half for oi in range(16)]
    own_tok = np.concatenate([np.arange(t * 128, (t + 1) * 128) for t in tiles])
    inv = (1.0 / 10000.0 ** (np.arange(16, dtype=np.float32) / 16)
           ).astype(np.float32)
    def tm(a, nt):
        a = a.reshape(nt, 128, 16)
        return np.ascontiguousarray(a.transpose(1, 0, 2)).reshape(128, nt * 16)
    ang_k = (np.arange(T, dtype=np.float32)[:, None] * inv[None, :]
             ).astype(np.float32)
    ang_q = ang_k[own_tok]
    ang_q64 = ang_q.astype(np.float64)
    ang_k64 = ang_k.astype(np.float64)
    cqt = tm(np.cos(ang_q64).astype(np.float32), 16)
    sqt = tm(np.sin(ang_q64).astype(np.float32), 16)
    ckt = tm(np.cos(ang_k64).astype(np.float32), 32)
    skt = tm(np.sin(ang_k64).astype(np.float32), 32)
    tri = np.where(np.arange(128)[:, None] >= np.arange(128)[None, :], 0.0,
                   NEG_BIG).astype(np.float32)
    m12 = np.zeros((128, 256), np.float32)
    if half == 0:
        m12[:, 0:128] = tri
        m12[:, 128:256] = NEG_BIG
    else:
        m12[:, 128:256] = tri
    return tiles, cqt, sqt, ckt, skt, m12


def wqr_perm():
    idx = np.arange(512)
    h, r = idx // 32, idx % 32
    half16, j = r // 16, r % 16
    new = np.empty(512, np.int64)
    new[half16 * 256 + h * 16 + j] = idx
    return new


class _Sem:
    __slots__ = ("h", "n", "by_eng", "name", "dma")

    def __init__(self, h, name, dma):
        self.h = h
        self.n = 0
        self.by_eng = {}
        self.name = name
        self.dma = dma


class Builder:
    def __init__(self, nc):
        self.nc = nc
        self.events = []
        self.ctx = []
        self.sems = []

    def sem(self, name, dma=False):
        cm = self.nc.semaphore(name)
        h = cm.__enter__()
        self.ctx.append(cm)
        sm = _Sem(h, name, dma)
        self.sems.append(sm)
        return sm

    def sb(self, name, shape, dt):
        cm = self.nc.sbuf_tensor(name, shape, dt)
        h = cm.__enter__()
        self.ctx.append(cm)
        return h

    def ev(self, eng, fn, waits=(), incs=()):
        wl = []
        for s in waits:
            if isinstance(s, tuple):
                s, v = s
            else:
                v = s.n
            if v <= 0:
                continue
            wl.append((s, v))
        for s, k in incs:
            s.n += k
            s.by_eng[eng] = s.by_eng.get(eng, 0) + k
        self.events.append((eng, tuple(wl), fn, tuple(incs)))

    def barrier(self, s_bar):
        tgt = s_bar.n + len(ENGINES)
        for eng in ENGINES:
            drains = tuple((sm.h, sm.n) for sm in self.sems
                           if sm is not s_bar and sm.by_eng.get(eng, 0) > 0
                           and sm.n > 0)

            def fn(e, _t=tgt, _d=drains):
                for h, v in _d:
                    e.wait_ge(h, v)
                e.sem_inc(s_bar.h, 1)
                e.wait_ge(s_bar.h, _t)
            self.events.append((eng, (), fn, ()))
        s_bar.n = tgt
        for eng in ENGINES:
            s_bar.by_eng[eng] = s_bar.by_eng.get(eng, 0) + 1

    def emit(self, blk):
        for eng_name in ENGINES:
            sec = getattr(blk, eng_name)
            evs = [e for e in self.events if e[0] == eng_name]

            def run(engine, _evs=evs):
                for _eng, waits, fn, incs in _evs:
                    for s, v in waits:
                        engine.wait_ge(s.h, v)
                    ins = fn(engine)
                    if incs:
                        for s, k in incs:
                            ins.then_inc(s.h, k)
            sec(run)


def build_program(nc, dbg=()):
    dbg = set(dbg)
    bld = Builder(nc)
    ev, sem, sb = bld.ev, bld.sem, bld.sb

    xbd = nc.dram_tensor("xb", [T, E], F32, kind="ExternalInput").ap()
    xqd = nc.dram_tensor("xq", [2048, E], F32, kind="ExternalInput").ap()
    wqd = nc.dram_tensor("wq", [E, E], F32, kind="ExternalInput").ap()
    wqrd = nc.dram_tensor("wqr", [E, 512], F32, kind="ExternalInput").ap()
    wkrd = nc.dram_tensor("wkr", [E, 32], F32, kind="ExternalInput").ap()
    wkvdd = nc.dram_tensor("wkvd", [E, 256], F32, kind="ExternalInput").ap()
    wkud = nc.dram_tensor("wku", [256, E], F32, kind="ExternalInput").ap()
    wvud = nc.dram_tensor("wvu", [256, E], F32, kind="ExternalInput").ap()
    wod = nc.dram_tensor("wo", [E, E], F32, kind="ExternalInput").ap()
    w1d = nc.dram_tensor("w1", [E, FF], F32, kind="ExternalInput").ap()
    w2d = nc.dram_tensor("w2", [FF, E], F32, kind="ExternalInput").ap()
    b1d = nc.dram_tensor("b1", [128, 32], F32, kind="ExternalInput").ap()
    b2bcd = nc.dram_tensor("b2bc", [128, E], F32, kind="ExternalInput").ap()
    lobod = nc.dram_tensor("lobo_bc", [128, H], F32, kind="ExternalInput").ap()
    cqtd = nc.dram_tensor("cqt", [128, 256], F32, kind="ExternalInput").ap()
    sqtd = nc.dram_tensor("sqt", [128, 256], F32, kind="ExternalInput").ap()
    cktd = nc.dram_tensor("ckt", [128, 512], F32, kind="ExternalInput").ap()
    sktd = nc.dram_tensor("skt", [128, 512], F32, kind="ExternalInput").ap()
    idnd = nc.dram_tensor("idn", [128, 128], F32, kind="ExternalInput").ap()
    m12d = nc.dram_tensor("m12", [128, 256], F32, kind="ExternalInput").ap()
    outd = nc.dram_tensor("out", [2048, E], F32, kind="ExternalOutput").ap()
    w1bf = nc.dram_tensor("w1bf", [E, FF], BF16, kind="Internal").ap()
    w2bf = nc.dram_tensor("w2bf", [FF, E], BF16, kind="Internal").ap()

    DBG = {}
    def dbg_t(name, shape, dt=BF16):
        if name in dbg:
            DBG[name] = nc.dram_tensor("dbg_" + name, shape, dt,
                                       kind="ExternalOutput").ap()
        return DBG.get(name)

    d_xnq = dbg_t("xnq", [128, 8 * 2048])
    d_lat = dbg_t("lat", [128, 2 * T])
    d_kr = dbg_t("kr", [32, T])
    d_qr = dbg_t("qr", [128, 4 * 2048])
    d_qc = dbg_t("qc", [128, 8 * 2048])
    d_tp = dbg_t("tp", [128, 32 * 512], F32)
    d_hq = dbg_t("hq", [128, 32 * 512])
    d_kasm = dbg_t("kasm", [128, T])
    d_vtok = dbg_t("vtok", [128, 32 * 64])
    d_E = dbg_t("E", [128, 2 * T])
    d_pt = dbg_t("pt", [128, 32 * 256])
    d_sc0 = dbg_t("sc0", [128, 512], F32)
    d_e2 = dbg_t("e2", [128, 2 * 512])
    d_qh = dbg_t("qh", [128, 2048])
    d_ka = dbg_t("ka", [128, T])
    d_den = dbg_t("den", [128, 16], F32)
    d_y = dbg_t("y", [128, 8 * 2048])

    s_cst = sem("cst", dma=True)
    s_wat = sem("wat", dma=True)
    s_wf = sem("wf", dma=True)
    s_xin = sem("xin", dma=True)
    s_tr = sem("tr", dma=True)
    s_xout = sem("xout", dma=True)
    s_w1l = sem("w1l", dma=True)
    s_w2l = sem("w2l", dma=True)
    s_vtr = sem("vtr", dma=True)
    s_etr = sem("etr", dma=True)
    s_oadd = sem("oadd", dma=True)
    s_wol = sem("wol", dma=True)
    s_rtr = sem("rtr", dma=True)
    s_dbg = sem("dbgs", dma=True)
    s_sq = sem("sq"); s_rcp = sem("rcp"); s_ln3 = sem("ln3")
    s_pmm = sem("pmm"); s_pev = sem("pev"); s_rope = sem("rope")
    s_bar = sem("bar")
    s_hmm = sem("hmm"); s_hev = sem("hev"); s_fv = sem("fv"); s_fg = sem("fg")
    s_fgp = sem("fgp"); s_fhq = sem("fhq"); s_omm = sem("omm"); s_oev = sem("oev")
    s_kmm = sem("kmm"); s_kev = sem("kev"); s_qcp = sem("qcp")
    s_smm = sem("smm"); s_sdg = sem("sdg"); s_sex = sem("sex"); s_red = sem("red")
    s_gmx = sem("gmx")
    s_fr = sem("fr")
    s_etp = [sem("etrA", dma=True), sem("etrB", dma=True)]
    s_pv = sem("pv"); s_pve = sem("pve")
    s_wmm = sem("wmm"); s_wev = sem("wev")
    s_vh = sem("vh")
    vh_n = [0]
    s_gh = sem("gh")
    gh_n = [0]

    def ghz(e, ins):
        gh_n[0] += 1
        ins.then_inc(s_gh.h, 1)
        e.wait_ge(s_gh.h, gh_n[0])

    def vhz(e, ins):
        vh_n[0] += 1
        ins.then_inc(s_vh.h, 1)
        e.wait_ge(s_vh.h, vh_n[0])

    latent = sb("latent", [128, 2, T], BF16)
    qr_rp = sb("qr_rp", [128, 4, 2048], BF16)
    kr_rp = sb("kr_rp", [32, T], BF16)
    qc_all = sb("qc_all", [128, 8, 2048], BF16)
    c_tile = sb("c_tile", [128, H], F32)
    m12 = sb("m12sb", [128, 256], F32)
    u_t = sb("u_t", [128, 64], F32)
    wku_sb = sb("wku_sb", [128, 2, E], BF16)
    wvu_sb = sb("wvu_sb", [128, 2, E], BF16)

    ps_cm = nc.psum_tensor("ps", [128, 4096], F32)
    PS = ps_cm.__enter__()
    bld.ctx.append(ps_cm)

    def groups_of(km):
        gs, s = [], 0
        while s < km:
            gs.append((s, min(1536, km - s)))
            s += 1536
        return gs

    for dst, src in ((m12[:], m12d), (c_tile[:], lobod)):
        ev("sync", (lambda e, d=dst, s=src: e.dma_start(out=d, in_=s)),
           incs=[(s_cst, 16)])

    xnq_cm = nc.sbuf_tensor("xnTq", [128, 8, 2048], BF16)
    XNQ = xnq_cm.__enter__()
    ab_cms = [
        nc.sbuf_tensor("xtile", [128, 2, E], F32),
        nc.sbuf_tensor("xnt", [128, 2, E], BF16),
        nc.sbuf_tensor("stat", [128, 16], F32),
        nc.sbuf_tensor("sq_scr", [128, 1024], F32),
        nc.sbuf_tensor("slab", [128, 2, 8, 512], BF16),
        nc.sbuf_tensor("cqt_sb", [128, 16, 16], F32),
        nc.sbuf_tensor("sqt_sb", [128, 16, 16], F32),
        nc.sbuf_tensor("ckt_sb", [128, 32, 16], F32),
        nc.sbuf_tensor("skt_sb", [128, 32, 16], F32),
        nc.sbuf_tensor("idn_sb", [128, 128], F32),
        nc.sbuf_tensor("qrt", [128, 512], F32),
        nc.sbuf_tensor("qrb", [128, 512], BF16),
        nc.sbuf_tensor("krt", [128, 128], F32),
        nc.sbuf_tensor("krb", [128, 128], F32),
        nc.sbuf_tensor("wkvd_sb", [128, 8, 256], BF16),
        nc.sbuf_tensor("wkr_sb", [128, 8, 32], BF16),
        nc.sbuf_tensor("wqr_sb", [128, 8, 512], BF16),
        nc.sbuf_tensor("wq_sb", [128, 8, 1024], BF16),
        nc.sbuf_tensor("prj", [128, 2, 512], F32),
    ]
    (xtile, xnt, stat, sq_scr, SLAB, CQT, SQT, CKT, SKT, IDN, QRT, QRB,
     KRT, KRB, WKVD, WKR, WQR, WQ, PRJ) = [cm.__enter__() for cm in ab_cms]


    for dst, src, n3 in ((CQT, cqtd, 16), (SQT, sqtd, 16), (CKT, cktd, 32),
                         (SKT, sktd, 32), (IDN, idnd, 0)):
        def fld(e, d=dst, s=src, n3=n3):
            o = d[:] if n3 == 0 else d[:, 0:n3, :].rearrange("p a b -> p (a b)")
            return e.dma_start(out=o, in_=s)
        ev("sync", fld, incs=[(s_cst, 16)])
    ev("scalar", lambda e: e.activation(c_tile[:], c_tile[:], AF.Exp),
       waits=[s_cst], incs=[(s_sq, 1)])

    for dc in range(8):
        for dst, src in ((WKVD, wkvdd), (WKR, wkrd), (WQR, wqrd), (WQ, wqd)):
            ev("gpsimd", (lambda e, d=dst, s=src, c=dc: e.dma_start(
                out=d[:, c, :], in_=s[c * 128:(c + 1) * 128, :])),
               incs=[(s_wat, 16)])
    for lc in range(2):
        for dst, src in ((wku_sb, wkud), (wvu_sb, wvud)):
            ev("gpsimd", (lambda e, d=dst, s=src, c=lc: e.dma_start(
                out=d[:, c, :], in_=s[c * 128:(c + 1) * 128, :])),
               incs=[(s_wat, 16)])
    def emit_wf_casts():
        for i in range(8):
            ev("gpsimd", (lambda e, i=i: e.dma_start(
                out=w1bf[i * 128:(i + 1) * 128, :],
                in_=w1d[i * 128:(i + 1) * 128, :])), incs=[(s_wf, 16)])
        for i in range(8):
            ev("gpsimd", (lambda e, i=i: e.dma_start(
                out=w2bf[i * 512:(i + 1) * 512, :],
                in_=w2d[i * 512:(i + 1) * 512, :])), incs=[(s_wf, 16)])


    ln_ctr = [0]

    def ln_tile(src_rows, tr_out, out_write=None, tr_waits=()):
        i = ln_ctr[0]
        ln_ctr[0] += 1
        pb = i % 2
        p = 8 * pb
        lw = [s_ln3, s_xout]
        if i >= 1:
            lw.append((s_sq, 2 * i))
        ev("sync", (lambda e: e.dma_start(out=xtile[:, pb, :], in_=src_rows)),
           waits=lw, incs=[(s_xin, 16)])
        ev("scalar", (lambda e: e.activation(
            sq_scr[:], xtile[:, pb, :], AF.Square,
            accum_out=stat[:, p + 1:p + 2])),
           waits=[s_xin], incs=[(s_sq, 1)])

        def fv(e):
            xt = xtile[:, pb, :]
            vhz(e, e.reduce_sum(stat[:, p:p + 1], xt, axis=AX.X))
            e.tensor_scalar_mul(stat[:, p + 2:p + 3], stat[:, p:p + 1], 1.0 / 1024.0)
            vhz(e, e.tensor_scalar_mul(stat[:, p + 3:p + 4],
                                       stat[:, p + 1:p + 2], 1.0 / 1024.0))
            vhz(e, e.tensor_tensor(stat[:, p + 4:p + 5], stat[:, p + 2:p + 3],
                                   stat[:, p + 2:p + 3], ALU.mult))
            vhz(e, e.tensor_tensor(stat[:, p + 3:p + 4], stat[:, p + 3:p + 4],
                                   stat[:, p + 4:p + 5], ALU.subtract))
            vhz(e, e.tensor_scalar_add(stat[:, p + 3:p + 4],
                                       stat[:, p + 3:p + 4], LN_EPS))
            return e.reciprocal(stat[:, p + 5:p + 6], stat[:, p + 3:p + 4])
        ev("vector", fv, waits=[s_xin, s_sq], incs=[(s_rcp, 1)])
        ev("scalar", (lambda e: e.activation(
            stat[:, p + 6:p + 7], stat[:, p + 5:p + 6], AF.Sqrt)),
           waits=[s_rcp], incs=[(s_sq, 1)])
        ev("vector", (lambda e: e.tensor_scalar(
            xnt[:, pb, :], xtile[:, pb, :], stat[:, p + 2:p + 3],
            stat[:, p + 6:p + 7], ALU.subtract, ALU.mult)),
           waits=[s_sq, s_tr], incs=[(s_ln3, 1)])
        ev("sync", (lambda e: e.dma_start(
            out=tr_out,
            in_=xnt[:, pb, :].rearrange("p (k c) -> p k c", c=128),
            transpose=True)),
           waits=[s_ln3] + list(tr_waits), incs=[(s_tr, 16)])
        if out_write is not None:
            ev("sync", (lambda e: e.dma_start(out=out_write, in_=xtile[:, pb, :])),
               waits=[s_ln3], incs=[(s_xout, 16)])

    jmm = [0]

    def proj_bank():
        bk = PS[:, (jmm[0] % 4) * 512:(jmm[0] % 4) * 512 + 512]
        need_wait = jmm[0] >= 4
        jmm[0] += 1
        return bk, need_wait

    for sg in range(8):
        sp = sg % 2
        for j in range(4):
            t = 4 * sg + j
            ln_tile(xbd[t * 128:(t + 1) * 128, :],
                    SLAB[:, sp, 0:8, j * 128:(j + 1) * 128], tr_waits=(s_pmm,))
        for a in (0, 1):
            bk, nw = proj_bank()

            def fmm(e, a=a, sp=sp, bk=bk):
                ins = None
                for dc in range(8):
                    ins = e.matmul(bk, WKVD[:, dc, a * 128:(a + 1) * 128],
                                   SLAB[:, sp, dc, :],
                                   start=(dc == 0), stop=(dc == 7))
                return ins
            w = [s_tr, s_wat] + ([s_pev] if nw else [])
            ev("tensor", fmm, waits=w, incs=[(s_pmm, 1)])
            ev("scalar", (lambda e, a=a, sg=sg, bk=bk: e.activation(
                latent[:, a, sg * 512:(sg + 1) * 512], bk, AF.Copy)),
               waits=[s_pmm], incs=[(s_pev, 1)])
        bk, nw = proj_bank()

        def fkrp(e, sp=sp, bk=bk):
            ins = None
            for j in range(4):
                for dc in range(8):
                    ins = e.matmul(bk[:, j * 32:(j + 1) * 32],
                                   SLAB[:, sp, dc, j * 128:(j + 1) * 128],
                                   WKR[:, dc, :],
                                   start=(dc == 0), stop=(dc == 7))
            return ins
        w = [s_tr, s_wat] + ([s_pev] if nw else [])
        ev("tensor", fkrp, waits=w, incs=[(s_pmm, 1)])
        ev("scalar", (lambda e, bk=bk: e.activation(KRT[:], bk[:, 0:128],
                                                    AF.Copy)),
           waits=[s_pmm, s_rope], incs=[(s_pev, 1)])

        def frope_k(e, sg=sg):
            v4 = lambda ap: ap.rearrange("p (t two j) -> p t two j", two=2, j=16)
            k1 = v4(KRT[:])[:, :, 0:1, :]
            k2 = v4(KRT[:])[:, :, 1:2, :]
            c = CKT[:, 4 * sg:4 * sg + 4, :].unsqueeze(2)
            sn = SKT[:, 4 * sg:4 * sg + 4, :].unsqueeze(2)
            t1 = v4(sq_scr[:, 0:128])[:, :, 0:1, :]
            t2 = v4(sq_scr[:, 128:256])[:, :, 0:1, :]
            r0 = v4(KRB[:])[:, :, 0:1, :]
            r1 = v4(KRB[:])[:, :, 1:2, :]
            vhz(e, e.tensor_tensor(t1, k1, c, ALU.mult))
            vhz(e, e.tensor_tensor(t2, k2, sn, ALU.mult))
            vhz(e, e.tensor_tensor(r0, t1, t2, ALU.subtract))
            vhz(e, e.tensor_tensor(t1, k2, c, ALU.mult))
            vhz(e, e.tensor_tensor(t2, k1, sn, ALU.mult))
            return e.tensor_tensor(r1, t1, t2, ALU.add)
        ev("vector", frope_k, waits=[s_pev, s_cst, s_rtr], incs=[(s_rope, 1)])
        bk2, nw2 = proj_bank()

        def fkrt(e, bk2=bk2):
            ins = None
            for j in range(4):
                ins = e.transpose(bk2[0:32, j * 128:(j + 1) * 128],
                                  KRB[:, j * 32:(j + 1) * 32], IDN[:])
            return ins
        ev("tensor", fkrt, waits=[s_rope, s_cst] + ([s_pev] if nw2 else []),
           incs=[(s_pmm, 1)])
        ev("scalar", (lambda e, sg=sg, bk2=bk2: e.activation(
            kr_rp[0:32, sg * 512:(sg + 1) * 512], bk2[0:32, :], AF.Copy)),
           waits=[s_pmm], incs=[(s_pev, 1)])

    for oi in range(16):
        ln_tile(xqd[oi * 128:(oi + 1) * 128, :],
                XNQ[:, 0:8, oi * 128:(oi + 1) * 128],
                out_write=outd[oi * 128:(oi + 1) * 128, :])

    if d_xnq is not None:
        for dc in range(8):
            ev("gpsimd", (lambda e, dc=dc: e.dma_start(
                out=d_xnq[:, dc * 2048:(dc + 1) * 2048], in_=XNQ[:, dc, :])),
               waits=[s_tr], incs=[(s_dbg, 16)])

    for oi in range(16):
        bk, nw = proj_bank()

        def fqr(e, oi=oi, bk=bk):
            ins = None
            for dc in range(8):
                ins = e.matmul(bk, XNQ[:, dc, oi * 128:(oi + 1) * 128],
                               WQR[:, dc, :], start=(dc == 0), stop=(dc == 7))
            return ins
        w = [s_tr, s_wat] + ([s_pev] if nw else [])
        ev("tensor", fqr, waits=w, incs=[(s_pmm, 1)])
        ev("scalar", (lambda e, bk=bk: e.activation(QRT[:], bk, AF.Copy)),
           waits=[s_pmm, s_rope], incs=[(s_pev, 1)])

        def frope_q(e, oi=oi):
            v4 = lambda ap: ap.rearrange("p (h two j) -> p h two j", two=2, j=16)
            q1 = v4(QRT[:])[:, :, 0:1, :]
            q2 = v4(QRT[:])[:, :, 1:2, :]
            bb = lambda ap: ap.unsqueeze(2).broadcast_to((128, 16, 1, 16))
            c = bb(CQT[:, oi:oi + 1, :])
            sn = bb(SQT[:, oi:oi + 1, :])
            t1 = v4(sq_scr[:, 0:512])[:, :, 0:1, :]
            t2 = v4(sq_scr[:, 512:1024])[:, :, 0:1, :]
            r0 = v4(QRB[:])[:, :, 0:1, :]
            r1 = v4(QRB[:])[:, :, 1:2, :]
            vhz(e, e.tensor_tensor(t1, q1, c, ALU.mult))
            vhz(e, e.tensor_tensor(t2, q2, sn, ALU.mult))
            vhz(e, e.tensor_tensor(r0, t1, t2, ALU.subtract))
            vhz(e, e.tensor_tensor(t1, q2, c, ALU.mult))
            vhz(e, e.tensor_tensor(t2, q1, sn, ALU.mult))
            return e.tensor_tensor(r1, t1, t2, ALU.add)
        ev("vector", frope_q, waits=[s_pev, s_cst, s_rtr], incs=[(s_rope, 1)])
        ev("sync", (lambda e, oi=oi: e.dma_start(
            out=qr_rp[:, 0:4, oi * 128:(oi + 1) * 128],
            in_=QRB[:].rearrange("p (k c) -> p k c", c=128), transpose=True)),
           waits=[s_rope], incs=[(s_rtr, 16)])

    for oc in range(8):
        for g in range(4):
            bk, nw = proj_bank()

            def fq(e, oc=oc, g=g, bk=bk):
                ins = None
                for dc in range(8):
                    ins = e.matmul(bk, WQ[:, dc, oc * 128:(oc + 1) * 128],
                                   XNQ[:, dc, g * 512:(g + 1) * 512],
                                   start=(dc == 0), stop=(dc == 7))
                return ins
            w = [s_tr, s_wat] + ([s_pev] if nw else [])
            ev("tensor", fq, waits=w, incs=[(s_pmm, 1)])
            ev("scalar", (lambda e, oc=oc, g=g, bk=bk: e.activation(
                qc_all[:, oc, g * 512:(g + 1) * 512], bk, AF.Copy)),
               waits=[s_pmm], incs=[(s_pev, 1)])

    if d_lat is not None:
        for a in range(2):
            ev("gpsimd", (lambda e, a=a: e.dma_start(
                out=d_lat[:, a * T:(a + 1) * T], in_=latent[:, a, :])),
               waits=[s_pev], incs=[(s_dbg, 16)])
    if d_kr is not None:
        ev("gpsimd", (lambda e: e.dma_start(out=d_kr, in_=kr_rp[:, :])),
           waits=[s_rope], incs=[(s_dbg, 16)])
    if d_qr is not None:
        for a in range(4):
            ev("gpsimd", (lambda e, a=a: e.dma_start(
                out=d_qr[:, a * 2048:(a + 1) * 2048], in_=qr_rp[:, a, :])),
               waits=[s_rtr], incs=[(s_dbg, 16)])
    if d_qc is not None:
        for a in range(8):
            ev("gpsimd", (lambda e, a=a: e.dma_start(
                out=d_qc[:, a * 2048:(a + 1) * 2048], in_=qc_all[:, a, :])),
               waits=[s_pev], incs=[(s_dbg, 16)])

    ev("sync", lambda e: e.nop(), waits=[s_xout, s_tr, s_rtr, s_cst, s_xin])
    ev("gpsimd", lambda e: e.nop(), waits=[s_dbg, s_wat])
    bld.barrier(s_bar)
    for cm in reversed(ab_cms):
        cm.__exit__(None, None, None)

    emit_wf_casts()

    c_cms = [
        nc.sbuf_tensor("HQ", [128, 32, 512], BF16),
        nc.sbuf_tensor("FT", [128, 16, 512], F32),
        nc.sbuf_tensor("W1C", [128, 2, 8, 128], BF16),
        nc.sbuf_tensor("W2C", [128, 2, 1024], BF16),
        nc.sbuf_tensor("FT2", [128, 4, 1024], F32),
        nc.sbuf_tensor("b2t", [128, E], F32),
        nc.sbuf_tensor("b1t", [128, 32], F32),
        nc.sbuf_tensor("cgel", [128, 2], F32),
    ]
    (HQ, FT, W1C, W2C, FT2, b2t, b1t,
     cgel) = [cm.__enter__() for cm in c_cms]

    ev("sync", (lambda e: e.dma_start(out=b2t[:], in_=b2bcd)),
       incs=[(s_cst, 16)])
    ev("sync", (lambda e: e.dma_start(out=b1t[:, 0:32], in_=b1d)),
       incs=[(s_cst, 16)])
    ev("vector", lambda e: e.tensor_scalar(b2t[:], b2t[:], 256.0,
                                           16384.0, ALU.mult, ALU.add),
       waits=[s_cst], incs=[(s_rcp, 1)])
    ev("vector", lambda e: e.tensor_scalar(b1t[:, 0:32], b1t[:, 0:32], 256.0,
                                           16384.0, ALU.mult, ALU.add),
       waits=[s_cst], incs=[(s_rcp, 1)])
    ev("vector", lambda e: e.memset(cgel[:, 0:1], -64.0),
       incs=[(s_rcp, 1)])
    ev("vector", lambda e: e.memset(cgel[:, 1:2], -63.75), incs=[(s_rcp, 1)])

    def ft(slot, p):
        return FT[:, 8 * p + slot, :]

    def w1_load(fc):
        def f(e, fc=fc):
            src = w1bf[:, fc * 128:(fc + 1) * 128].rearrange(
                "(dc p) f -> p dc f", p=128)
            return e.dma_start(out=W1C[:, fc % 2, :, :], in_=src)
        ev("sync", f, waits=[s_wf, s_hmm], incs=[(s_w1l, 16)])

    hev_hist, fv_hist, fgp_hist, fhq_hist = {}, {}, {}, {}
    ommg, w2ln = {}, {}
    pend_v = [None]
    pend_s = [None]
    from collections import deque
    pend_o = deque()

    def drain_o(nmax):
        c = 0
        while pend_o and c < nmax:
            pend_o.popleft()()
            c += 1

    def w2_load_g(odx, fcl):
        def f(e, odx=odx, fcl=fcl):
            return e.dma_start(out=W2C[:, odx % 2, :],
                               in_=w2bf[fcl * 128:(fcl + 1) * 128, :])
        ev("sync", f, waits=[s_wf, (s_w2l, s_w2l.n)] +
           ([(s_omm, ommg[odx - 2])] if odx >= 2 else []),
           incs=[(s_w2l, 16)])
        w2ln[odx] = s_w2l.n

    def emit_foq(tg, P):
        for tcl in range(2):
            tc = 2 * P + tcl

            def foq(e, tcl=tcl, tc=tc):
                A = FT2[:, 0, :]
                Bf = FT2[:, 1, :]
                OUT = FT2[:, 2 + tc % 2, :]
                psl = PS[:, 2048 + tcl * 1024:2048 + (tcl + 1) * 1024]
                vhz(e, e.scalar_tensor_tensor(A, psl, 256.0, b2t[:],
                                              ALU.mult, ALU.add))
                vhz(e, e.tensor_scalar(Bf, A, MAGIC, -MAGIC,
                                       ALU.add, ALU.add))
                vhz(e, e.scalar_tensor_tensor(A, A, -0.5, Bf,
                                              ALU.add, ALU.max))
                return e.tensor_scalar(OUT, A, 1.0 / 256.0, -64.0,
                                       ALU.mult, ALU.add)
            ev("vector", foq, waits=[s_omm, s_oadd, s_rcp],
               incs=[(s_oev, 1)])
            oi = tg * 4 + tc
            ev("gpsimd", (lambda e, oi=oi, tc=tc: e.dma_start(
                out=outd[oi * 128:(oi + 1) * 128, :],
                in_=FT2[:, 2 + tc % 2, :], accum_op=ALU.add)),
               waits=[s_oev, s_xout], incs=[(s_oadd, 16)])

    def mk_fo(tg, P, fc):
        odx = tg * 64 + P * 32 + fc
        jb = tg * 32 + fc

        def go():
            if fc == 0:
                w2_load_g(odx, fc)
                w2_load_g(odx + 1, fc + 1)

            def fo(e, tg=tg, P=P, fc=fc, odx=odx):
                ins = None
                for tcl in range(2):
                    tc = 2 * P + tcl
                    for oh in range(2):
                        ins = e.matmul(
                            PS[:, 2048 + tcl * 1024 + oh * 512:
                               2048 + tcl * 1024 + oh * 512 + 512],
                            HQ[:, fc, tc * 128:(tc + 1) * 128],
                            W2C[:, odx % 2, oh * 512:(oh + 1) * 512],
                            start=(fc == 0), stop=(fc == 31))
                return ins
            ev("tensor", fo,
               waits=[(s_fhq, fhq_hist[jb]), s_oev, (s_w2l, w2ln[odx])],
               incs=[(s_omm, 1)])
            ommg[odx] = s_omm.n
            if fc + 2 < 32:
                w2_load_g(odx + 2, fc + 2)
            if fc == 31:
                emit_foq(tg, P)
        return go

    for tg in range(4):
        w1_load(0)
        w1_load(1)
        for fc in range(32):
            p = fc % 2
            jb = tg * 32 + fc
            bkh = PS[:, (jb % 4) * 512:(jb % 4) * 512 + 512]

            def fh(e, tg=tg, fc=fc, bkh=bkh):
                ins = None
                for dc in range(8):
                    ins = e.matmul(bkh, W1C[:, fc % 2, dc, :],
                                   XNQ[:, dc, tg * 512:(tg + 1) * 512],
                                   start=(dc == 0), stop=(dc == 7))
                return ins
            whl = [s_tr]
            if jb >= 4:
                whl.append((s_hev, hev_hist[jb - 4]))
            if fc < 2:
                whl.append((s_w1l, s_w1l.n))
            else:
                whl.append(s_w1l)
            ev("tensor", fh, waits=whl, incs=[(s_hmm, 1)])
            drain_o(2)
            if fc + 2 < 32:
                w1_load(fc + 2)

            ev("scalar", (lambda e, fc=fc, p=p, bkh=bkh: e.activation(
                ft(0, p), bkh, AF.Identity, bias=b1t[:, fc:fc + 1],
                scale=256.0)), waits=[s_hmm, s_rcp] +
               ([(s_fv, fv_hist[jb - 2])] if jb >= 2 else []),
               incs=[(s_hev, 1)])
            hev_hist[jb] = s_hev.n
            if pend_s[0] is not None:
                pend_s[0]()
                pend_s[0] = None

            if tg == 0 and d_tp is not None:
                ev("gpsimd", (lambda e, fc=fc, p=p: e.dma_start(
                    out=d_tp[:, fc * 512:(fc + 1) * 512], in_=ft(0, p))),
                   waits=[s_hev], incs=[(s_dbg, 16)])

            def fvc1(e, p=p):
                vhz(e, e.tensor_scalar(ft(1, p), ft(0, p), MAGIC, -MAGIC,
                                       ALU.add, ALU.add))
                vhz(e, e.scalar_tensor_tensor(ft(2, p), ft(0, p), -0.5,
                                              ft(1, p), ALU.add, ALU.max))
                vhz(e, e.tensor_scalar(ft(3, p), ft(2, p), 1.0 / 64.0, -0.5,
                                       ALU.mult, ALU.add))
                vhz(e, e.tensor_scalar(ft(1, p), ft(3, p), MAGIC, -MAGIC,
                                       ALU.add, ALU.add))
                vhz(e, e.scalar_tensor_tensor(ft(4, p), ft(3, p), -0.5,
                                              ft(1, p), ALU.add, ALU.max))
                return e.scalar_tensor_tensor(ft(5, p), ft(4, p), -64.0,
                                              ft(2, p), ALU.mult, ALU.add)
            ev("vector", fvc1, waits=[s_hev] +
               ([(s_fgp, fgp_hist[jb - 2])] if jb >= 2 else []) +
               ([s_dbg] if (tg == 0 and d_tp is not None) else []),
               incs=[(s_fv, 1)])
            fv_hist[jb] = s_fv.n
            if pend_v[0] is not None:
                pend_v[0]()
                pend_v[0] = None

            def q_gel(p=p, jb=jb):
                def fgel(e, p=p):
                    e.activation(ft(6, p), ft(4, p), AF.Gelu,
                                 bias=cgel[:, 0:1], scale=0.25)
                    return e.activation(ft(7, p), ft(4, p), AF.Gelu,
                                        bias=cgel[:, 1:2], scale=0.25)
                ev("scalar", fgel, waits=[(s_fv, fv_hist[jb]), s_rcp],
                   incs=[(s_fg, 1)])

                def fgps(e, p=p):
                    ghz(e, e.tensor_tensor(ft(3, p), ft(7, p), ft(6, p),
                                           ALU.subtract))
                    return e.tensor_tensor(ft(1, p), ft(3, p), ft(5, p),
                                           ALU.mult)
                ev("gpsimd", fgps, waits=[s_fg, (s_fv, fv_hist[jb])] +
                   ([(s_fhq, fhq_hist[jb - 2])] if jb >= 2 else []),
                   incs=[(s_fgp, 1)])
                fgp_hist[jb] = s_fgp.n
            pend_s[0] = q_gel

            def q_vc2(fc=fc, p=p, jb=jb, tg=tg):
                def fvc2(e, fc=fc, p=p):
                    vhz(e, e.tensor_scalar(ft(3, p), ft(1, p), 4.0, 16384.0,
                                           ALU.mult, ALU.add))
                    vhz(e, e.scalar_tensor_tensor(ft(2, p), ft(6, p), 256.0,
                                                  ft(3, p), ALU.mult,
                                                  ALU.add))
                    vhz(e, e.tensor_scalar(ft(3, p), ft(2, p), MAGIC, -MAGIC,
                                           ALU.add, ALU.add))
                    vhz(e, e.scalar_tensor_tensor(ft(4, p), ft(2, p), -0.5,
                                                  ft(3, p), ALU.add, ALU.max))
                    return e.tensor_scalar(HQ[:, fc, :], ft(4, p), 1.0 / 256.0,
                                           -64.0, ALU.mult, ALU.add)
                ev("vector", fvc2, waits=[(s_fgp, fgp_hist[jb])] +
                   ([(s_omm, ommg[(tg - 1) * 64 + 32 + fc])] if tg >= 1
                    else []),
                   incs=[(s_fhq, 1)])
                fhq_hist[jb] = s_fhq.n
                pend_o.append(mk_fo(tg, 0, fc))
                if fc == 31:
                    for fc2 in range(32):
                        pend_o.append(mk_fo(tg, 1, fc2))
            pend_v[0] = q_vc2

        if tg == 0 and d_hq is not None:
            for fc in range(32):
                ev("gpsimd", (lambda e, fc=fc: e.dma_start(
                    out=d_hq[:, fc * 512:(fc + 1) * 512], in_=HQ[:, fc, :])),
                   waits=[s_fhq], incs=[(s_dbg, 16)])

        if pend_s[0] is not None:
            pend_s[0]()
            pend_s[0] = None
        if pend_v[0] is not None:
            pend_v[0]()
            pend_v[0] = None

    while pend_o:
        pend_o.popleft()()

    ev("sync", lambda e: e.nop(), waits=[s_w1l, s_w2l])
    ev("gpsimd", lambda e: e.nop(), waits=[s_oadd, s_dbg, s_wf])
    bld.barrier(s_bar)
    for cm in reversed(c_cms):
        cm.__exit__(None, None, None)
    xnq_cm.__exit__(None, None, None)

    y_cm = nc.sbuf_tensor("y_sb", [128, 8, 2048], BF16)
    y_sb = y_cm.__enter__()
    d_cms = [
        nc.sbuf_tensor("kasm", [128, 2, T], BF16),
        nc.sbuf_tensor("vfeat", [64, T], BF16),
        nc.sbuf_tensor("vtok", [128, 2, 32, 64], BF16),
        nc.sbuf_tensor("qh", [128, 2, 2048], BF16),
        nc.sbuf_tensor("Ebuf", [128, 3, T], BF16),
        nc.sbuf_tensor("PT", [128, 60, 256], BF16),
    ]
    KA, VF, VT, QH, EB, PT = [cm.__enter__() for cm in d_cms]


    gctr = [0]
    kev_i = [0]
    pend_pv = [None]
    rowq = []
    smm_hd, vtr_hd, pv_hd = {}, {}, {}
    sex_g, sdg_g = {}, {}
    etr_pr, pve_pr, red_pr, pv_pr = {}, {}, {}, {}

    for h in range(H):
        hp = h % 2
        for wsb, is_k in ((wku_sb, True), (wvu_sb, False)):
            for g in range(8):
                i = kev_i[0]
                slot = i % 2

                def fkc(e, h=h, g=g, slot=slot, wsb=wsb):
                    ins = None
                    for lc in range(2):
                        ins = e.matmul(PS[slot * 64:slot * 64 + 64, 3072:3584],
                                       wsb[:, lc, h * 64:(h + 1) * 64],
                                       latent[:, lc, g * 512:(g + 1) * 512],
                                       start=(lc == 0), stop=(lc == 1))
                    return ins
                ev("tensor", fkc, waits=[s_wat] +
                   ([(s_kev, i - 1)] if i >= 2 else []), incs=[(s_kmm, 1)])
                if is_k:
                    dst = KA[0:64, hp, g * 512:(g + 1) * 512]
                    lagw = [(s_smm, smm_hd[h - 2])] if h >= 2 else []
                else:
                    dst = VF[:, g * 512:(g + 1) * 512]
                    lagw = [(s_vtr, vtr_hd[h - 1])] if h >= 1 else []
                ev("scalar", (lambda e, dst=dst, slot=slot: e.activation(
                    dst, PS[slot * 64:slot * 64 + 64, 3072:3584], AF.Copy)),
                   waits=[s_kmm] + lagw, incs=[(s_kev, 1)])
                kev_i[0] += 1
        ev("sync", (lambda e, hp=hp: e.dma_start(
            out=VT[:, hp, :, :],
            in_=VF[:, :].rearrange("p (k c) -> p k c", c=128),
            transpose=True)),
           waits=[s_kev, (s_vtr, s_vtr.n)] +
           ([(s_pv, pv_hd[h - 2])] if h >= 2 else []),
           incs=[(s_vtr, 16)])
        vtr_hd[h] = s_vtr.n
        if h < 2:
            ev("vector", (lambda e, hp=hp: e.tensor_copy(
                KA[64:96, hp, :], kr_rp[:, :])),
               waits=[s_rope], incs=[(s_qcp, 1)])

        def fqa(e, h=h, hp=hp):
            e.tensor_copy(QH[0:64, hp, :],
                          qc_all[(h % 2) * 64:(h % 2) * 64 + 64, h // 2, :])
            return e.tensor_copy(QH[64:96, hp, :],
                                 qr_rp[(h % 4) * 32:(h % 4) * 32 + 32,
                                       h // 4, :])
        ev("vector", fqa, waits=[s_qcp] +
           ([(s_smm, smm_hd[h - 2])] if h >= 2 else []),
           incs=[(s_qcp, 1)])
        if h == 0 and d_qh is not None:
            ev("gpsimd", (lambda e: e.dma_start(out=d_qh, in_=QH[:, 0, :])),
               waits=[s_qcp], incs=[(s_dbg, 16)])
        if h == 0 and d_ka is not None:
            ev("gpsimd", (lambda e: e.dma_start(out=d_ka, in_=KA[:, 0, :])),
               waits=[s_qcp, s_kev], incs=[(s_dbg, 16)])

        for pr in range(8):
            k = h * 8 + pr
            oiA, oiB = 2 * pr, 2 * pr + 1
            for rr, oi in ((0, oiA), (1, oiB)):
                km = KMU[oi]
                grps = groups_of(km)
                slotE = (2 * k + rr) % 3
                ub = (pr % 2) * 32 + rr * 16
                mb = [(km - 256, 0), (km - 128, 128)]
                for gi, (gs, gl) in enumerate(grps):
                    g = gctr[0]
                    tb = 3 * (g % 2)
                    gctr[0] += 1

                    def fsc(e, hp=hp, oi=oi, gs=gs, gl=gl, tb=tb):
                        ins = None
                        m = 0
                        while m * 512 < gl:
                            wd = min(512, gl - m * 512)
                            ins = e.matmul(
                                PS[:, (tb + m) * 512:(tb + m) * 512 + wd],
                                QH[0:96, hp, oi * 128:(oi + 1) * 128],
                                KA[0:96, hp, gs + m * 512:gs + m * 512 + wd],
                                start=True, stop=True)
                            m += 1
                        return ins
                    ev("tensor", fsc, waits=[s_qcp, s_kev] +
                       ([(s_sex, sex_g[g - 2]), (s_sdg, sdg_g[g - 2])]
                        if g >= 2 else []) +
                       ([s_dbg] if (h == 0 and d_qh is not None) else []),
                       incs=[(s_smm, 1)])
                    nmask = 0
                    for pos, mcol in mb:
                        if gs <= pos < gs + gl:
                            off = pos - gs

                            def fmask(e, tb=tb, off=off, mcol=mcol):
                                return e.tensor_tensor(
                                    PS[:, tb * 512 + off:tb * 512 + off + 128],
                                    PS[:, tb * 512 + off:tb * 512 + off + 128],
                                    m12[:, mcol:mcol + 128], ALU.add)
                            ev("vector", fmask, waits=[s_smm] +
                               ([s_dbg] if (h == 0 and pr == 0 and rr == 0
                                            and d_sc0 is not None) else []),
                               incs=[(s_sdg, 1)])
                            nmask += 1

                    def fex(e, slotE=slotE, gs=gs, gl=gl, tb=tb, ub=ub,
                            gi=gi):
                        return e.activation(
                            EB[:, slotE, gs:gs + gl],
                            PS[:, tb * 512:tb * 512 + gl],
                            AF.Exp, scale=SCALE96,
                            accum_out=u_t[:, ub + gi:ub + gi + 1])
                    klag = k - 2 if rr == 0 else k - 1
                    ev("scalar", fex, waits=[s_smm, s_sdg] +
                       ([etr_pr[klag]] if klag >= 0 else []) +
                       ([(s_red, red_pr[k - 2])] if k >= 2 else []),
                       incs=[(s_sex, 1)])
                    sex_g[g] = s_sex.n
                    sdg_g[g] = s_sdg.n
                    ev("vector", (lambda e, slotE=slotE, gs=gs, gl=gl,
                                  ub=ub, gi=gi: e.reduce_max(
                                      u_t[:, ub + 3 + gi:ub + 4 + gi],
                                      EB[:, slotE, gs:gs + gl], axis=AX.X)),
                       waits=[s_sex], incs=[(s_gmx, 1)])

                if h == 0 and pr == 0 and d_e2 is not None:
                    ev("gpsimd", (lambda e, rr=rr, km=km: e.dma_start(
                        out=d_e2[:, rr * 512:rr * 512 + km],
                        in_=EB[:, rr, 0:km])),
                       waits=[s_sex], incs=[(s_dbg, 16)])

                def fred(e, km=km, ng=len(grps), h=h, ub=ub):
                    r8 = ub
                    if ng > 1:
                        vhz(e, e.reduce_sum(u_t[:, r8 + 6:r8 + 7],
                                            u_t[:, r8:r8 + ng], axis=AX.X))
                        vhz(e, e.reduce_max(u_t[:, r8 + 7:r8 + 8],
                                            u_t[:, r8 + 3:r8 + 3 + ng],
                                            axis=AX.X))
                    else:
                        vhz(e, e.tensor_copy(u_t[:, r8 + 6:r8 + 7],
                                             u_t[:, r8:r8 + 1]))
                        vhz(e, e.tensor_copy(u_t[:, r8 + 7:r8 + 8],
                                             u_t[:, r8 + 3:r8 + 4]))
                    vhz(e, e.scalar_tensor_tensor(u_t[:, r8 + 8:r8 + 9],
                                                  u_t[:, r8 + 7:r8 + 8],
                                                  c_tile[:, h:h + 1],
                                                  u_t[:, r8 + 6:r8 + 7],
                                                  ALU.mult, ALU.add))
                    return e.reciprocal(u_t[:, r8 + 9:r8 + 10],
                                        u_t[:, r8 + 8:r8 + 9])
                ev("vector", fred, waits=[s_sex, s_gmx] +
                   ([s_dbg] if (h == 0 and pr == 0 and d_e2 is not None)
                    else []), incs=[(s_fr, 1)])
                rowq.append((slotE, km, ub, s_fr.n))
                if rr == 0:
                    ev("vector", (lambda e, kmA=KMU[oiA], kmB=KMU[oiB], k=k:
                                  e.memset(EB[:, (2 * k) % 3, kmA:kmB], 0.0)),
                       waits=([etr_pr[k - 2]] if k >= 2 else []),
                       incs=[(s_red, 1)])

            red_rr = []
            for slotE_, km_, ub_, frn_ in rowq:
                ev("scalar", (lambda e, slotE=slotE_, km=km_, ub=ub_:
                              e.activation(EB[:, slotE, 0:km],
                                           EB[:, slotE, 0:km], AF.Identity,
                                           scale=u_t[:, ub + 9:ub + 10])),
                   waits=[(s_fr, frn_)], incs=[(s_red, 1)])
                red_rr.append(s_red.n)
            rowq.clear()
            Kb = KMU[oiB] // 128
            pto = 0 if k % 2 == 0 else 28
            sp = s_etp[k % 2]
            for rr in range(2):
                ev("sync", (lambda e, rr=rr, K=Kb, k=k, pto=pto: e.dma_start(
                    out=PT[:, pto:pto + K, rr * 128:rr * 128 + 128],
                    in_=EB[:, (2 * k + rr) % 3, 0:K * 128]
                    .rearrange("p (k c) -> p k c", c=128),
                    transpose=True)),
                   waits=[(s_red, red_rr[rr]), (sp, sp.n)] +
                   ([(s_pv, pv_pr[k - 2])] if k >= 2 else []),
                   incs=[(sp, 16)])

            if d_den is not None and h == 0 and pr == 0:
                ev("gpsimd", lambda e: e.dma_start(out=d_den, in_=u_t[:]),
                   waits=[s_red], incs=[(s_dbg, 16)])

            kmA, kmB = KMU[oiA], KMU[oiB]
            red_pr[k] = s_red.n
            etr_pr[k] = (s_etp[k % 2], s_etp[k % 2].n)

            if h == 0 and pr == 0 and d_E is not None:
                for rr in range(2):
                    ev("gpsimd", (lambda e, rr=rr: e.dma_start(
                        out=d_E[:, rr * T:rr * T + T], in_=EB[:, rr, :])),
                       waits=[s_red], incs=[(s_dbg, 16)])
            K = kmB // 128
            pto = 0 if k % 2 == 0 else 28

            if pend_pv[0] is not None:
                pend_pv[0]()
                pend_pv[0] = None

            def emit_pv(hp=hp, K=K, pr=pr, pto=pto, k=k, h=h,
                        etr_tgt=etr_pr[k]):
                def fpv(e, hp=hp, K=K, pr=pr, pto=pto):
                    p = (pr % 2) * 64
                    ins = None
                    for kc in range(K):
                        ins = e.matmul(PS[p:p + 64, 3584:3840],
                                       VT[:, hp, kc, :],
                                       PT[:, pto + kc, :], start=(kc == 0),
                                       stop=(kc == K - 1))
                    return ins
                ev("tensor", fpv, waits=[etr_tgt, s_vtr] +
                   ([(s_pve, pve_pr[k - 2])] if k >= 2 else []),
                   incs=[(s_pv, 1)])
                pv_pr[k] = s_pv.n
                ev("scalar", (lambda e, h=h, pr=pr: e.activation(
                    y_sb[(h % 2) * 64:(h % 2) * 64 + 64, h // 2,
                         pr * 256:(pr + 1) * 256],
                    PS[(pr % 2) * 64:(pr % 2) * 64 + 64, 3584:3840], AF.Copy)),
                   waits=[s_pv], incs=[(s_pve, 1)])
                pve_pr[k] = s_pve.n
            pend_pv[0] = emit_pv

        if pend_pv[0] is not None:
            pend_pv[0]()
            pend_pv[0] = None
        smm_hd[h] = s_smm.n
        pv_hd[h] = s_pv.n
        if h == 0:
            if d_kasm is not None:
                ev("gpsimd", (lambda e: e.dma_start(out=d_kasm, in_=KA[:, 0, :])),
                   waits=[s_kev, s_qcp], incs=[(s_dbg, 16)])
            if d_vtok is not None:
                ev("gpsimd", (lambda e: e.dma_start(
                    out=d_vtok,
                    in_=VT[:, 0, :, :].rearrange("p k c -> p (k c)"))),
                   waits=[s_vtr], incs=[(s_dbg, 16)])


    if d_y is not None:
        for a in range(8):
            ev("gpsimd", (lambda e, a=a: e.dma_start(
                out=d_y[:, a * 2048:(a + 1) * 2048], in_=y_sb[:, a, :])),
               waits=[s_pve], incs=[(s_dbg, 16)])

    ev("sync", lambda e: e.nop(),
       waits=[s_etp[0], s_etp[1], s_vtr])
    ev("gpsimd", lambda e: e.nop(), waits=[s_dbg])
    bld.barrier(s_bar)
    for cm in reversed(d_cms):
        cm.__exit__(None, None, None)

    e_cms = [
        nc.sbuf_tensor("WO", [128, 8, 1024], BF16),
        nc.sbuf_tensor("OB", [128, 2, 1024], F32),
    ]
    WOsb, OB = [cm.__enter__() for cm in e_cms]

    for prn in range(8):
        ev("gpsimd", (lambda e, prn=prn: e.dma_start(
            out=WOsb[:, prn, :], in_=wod[prn * 128:(prn + 1) * 128, :])),
           incs=[(s_wol, 16)])

    for oi in range(16):
        pb = oi % 2

        def fwo(e, oi=oi, pb=pb):
            ins = None
            for prn in range(8):
                for oh in range(2):
                    ins = e.matmul(
                        PS[:, pb * 1024 + oh * 512:pb * 1024 + oh * 512 + 512],
                        y_sb[:, prn, oi * 128:(oi + 1) * 128],
                        WOsb[:, prn, oh * 512:(oh + 1) * 512],
                        start=(prn == 0), stop=(prn == 7))
            return ins
        ev("tensor", fwo, waits=[s_wol, s_pve, s_wev], incs=[(s_wmm, 1)])
        ev("vector", (lambda e, pb=pb: e.tensor_copy(
            OB[:, pb, :], PS[:, pb * 1024:(pb + 1) * 1024])),
           waits=[s_wmm, s_oadd], incs=[(s_wev, 1)])
        ev("gpsimd", (lambda e, oi=oi, pb=pb: e.dma_start(
            out=outd[oi * 128:(oi + 1) * 128, :], in_=OB[:, pb, :],
            accum_op=ALU.add)),
           waits=[s_wev], incs=[(s_oadd, 16)])

    ev("gpsimd", lambda e: e.nop(), waits=[s_oadd, s_dbg])
    ev("sync", lambda e: e.nop(), waits=[s_xout])
    for cm in reversed(e_cms):
        cm.__exit__(None, None, None)
    y_cm.__exit__(None, None, None)

    return bld


def kernel(L0_attn_x, L0_attn_Wq, L0_attn_Wqr, L0_attn_Wkr, L0_attn_Wkvd,
           L0_attn_Wku, L0_attn_Wvu, L0_attn_Wo, L0_attn_lobo_log,
           L0_ffn_w1, L0_ffn_b1, L0_ffn_w2, L0_ffn_b2, _dbg=(), _cores=8):
    x = np.ascontiguousarray(np.asarray(L0_attn_x, dtype=np.float32))
    base = {
        "wq": np.ascontiguousarray(np.asarray(L0_attn_Wq, np.float32)),
        "wqr": np.ascontiguousarray(np.asarray(L0_attn_Wqr, np.float32)),
        "wkr": np.ascontiguousarray(np.asarray(L0_attn_Wkr, np.float32)),
        "wkvd": np.ascontiguousarray(np.asarray(L0_attn_Wkvd, np.float32)),
        "wku": np.ascontiguousarray(np.asarray(L0_attn_Wku, np.float32)),
        "wvu": np.ascontiguousarray(np.asarray(L0_attn_Wvu, np.float32)),
        "wo": np.ascontiguousarray(np.asarray(L0_attn_Wo, np.float32)),
        "w1": np.ascontiguousarray(np.asarray(L0_ffn_w1, np.float32)),
        "w2": np.ascontiguousarray(np.asarray(L0_ffn_w2, np.float32)),
        "b1": np.ascontiguousarray(
            np.asarray(L0_ffn_b1, np.float32).reshape(32, 128).T),
        "b2bc": np.ascontiguousarray(
            np.tile(np.asarray(L0_ffn_b2, np.float32)[None, :], (128, 1))),
        "lobo_bc": np.ascontiguousarray(
            np.tile(np.asarray(L0_attn_lobo_log, np.float32)[None, :],
                    (128, 1))),
    }
    halves = [host_tables(0), host_tables(1)]

    nc = bass.Bass("TRN2", target_bir_lowering=False)
    bld = build_program(nc, dbg=_dbg)

    in_maps = []
    idn = np.eye(128, dtype=np.float32)
    for c in range(_cores):
        b, half = c // 2, c % 2
        tiles, cqt, sqt, ckt, skt, m12 = halves[half]
        own_rows = np.concatenate(
            [x[b, t * 128:(t + 1) * 128, :] for t in tiles], axis=0)
        m = dict(base)
        m["xb"] = np.ascontiguousarray(x[b])
        m["xq"] = np.ascontiguousarray(own_rows)
        m["cqt"] = cqt
        m["sqt"] = sqt
        m["ckt"] = ckt
        m["skt"] = skt
        m["idn"] = idn
        m["m12"] = m12
        in_maps.append(m)

    with nc.Block() as blk:
        bld.emit(blk)

    res = run_bass_kernel_spmd(nc, in_maps, core_ids=list(range(_cores)))

    outf = np.empty((B, T, E), np.asarray(res.results[0]["out"]).dtype)
    for c in range(_cores):
        b, half = c // 2, c % 2
        tiles = halves[half][0]
        o = np.asarray(res.results[c]["out"])
        for oi, t in enumerate(tiles):
            outf[b, t * 128:(t + 1) * 128, :] = o[oi * 128:(oi + 1) * 128, :]
    if _dbg:
        kernel._dbg_results = res.results
    return outf




# version 19
# speedup vs baseline: 1.0880x; 1.0880x over previous
import numpy as np
import ml_dtypes
import concourse.bass as bass
import concourse.mybir as mybir
from concourse.bass_utils import run_bass_kernel_spmd

F32 = mybir.dt.float32
BF16 = mybir.dt.bfloat16
AF = mybir.ActivationFunctionType
ALU = mybir.AluOpType
AX = mybir.AxisListType

B, T, E = 4, 4096, 1024
H = 16
FF = 4096
MAGIC = float(2.0 ** 52)
SCALE96 = float(np.float32(1.0) / np.sqrt(np.float32(96.0), dtype=np.float32))
NEG_BIG = -1.0e30
LN_EPS = 1e-5
ENGINES = ("sync", "tensor", "vector", "scalar", "gpsimd")
KMU = [256 * (oi + 1) for oi in range(16)]


def host_tables(half):
    tiles = [2 * oi + half for oi in range(16)]
    own_tok = np.concatenate([np.arange(t * 128, (t + 1) * 128) for t in tiles])
    inv = (1.0 / 10000.0 ** (np.arange(16, dtype=np.float32) / 16)
           ).astype(np.float32)
    def tm(a, nt):
        a = a.reshape(nt, 128, 16)
        return np.ascontiguousarray(a.transpose(1, 0, 2)).reshape(128, nt * 16)
    ang_k = (np.arange(T, dtype=np.float32)[:, None] * inv[None, :]
             ).astype(np.float32)
    ang_q = ang_k[own_tok]
    ang_q64 = ang_q.astype(np.float64)
    ang_k64 = ang_k.astype(np.float64)
    cqt = tm(np.cos(ang_q64).astype(np.float32), 16)
    sqt = tm(np.sin(ang_q64).astype(np.float32), 16)
    ckt = tm(np.cos(ang_k64).astype(np.float32), 32)
    skt = tm(np.sin(ang_k64).astype(np.float32), 32)
    tri = np.where(np.arange(128)[:, None] >= np.arange(128)[None, :], 0.0,
                   NEG_BIG).astype(np.float32)
    m12 = np.zeros((128, 256), np.float32)
    if half == 0:
        m12[:, 0:128] = tri
        m12[:, 128:256] = NEG_BIG
    else:
        m12[:, 128:256] = tri
    return tiles, cqt, sqt, ckt, skt, m12


def wqr_perm():
    idx = np.arange(512)
    h, r = idx // 32, idx % 32
    half16, j = r // 16, r % 16
    new = np.empty(512, np.int64)
    new[half16 * 256 + h * 16 + j] = idx
    return new


class _Sem:
    __slots__ = ("h", "n", "by_eng", "name", "dma")

    def __init__(self, h, name, dma):
        self.h = h
        self.n = 0
        self.by_eng = {}
        self.name = name
        self.dma = dma


class Builder:
    def __init__(self, nc):
        self.nc = nc
        self.events = []
        self.ctx = []
        self.sems = []

    def sem(self, name, dma=False):
        cm = self.nc.semaphore(name)
        h = cm.__enter__()
        self.ctx.append(cm)
        sm = _Sem(h, name, dma)
        self.sems.append(sm)
        return sm

    def sb(self, name, shape, dt):
        cm = self.nc.sbuf_tensor(name, shape, dt)
        h = cm.__enter__()
        self.ctx.append(cm)
        return h

    def ev(self, eng, fn, waits=(), incs=()):
        wl = []
        for s in waits:
            if isinstance(s, tuple):
                s, v = s
            else:
                v = s.n
            if v <= 0:
                continue
            wl.append((s, v))
        for s, k in incs:
            s.n += k
            s.by_eng[eng] = s.by_eng.get(eng, 0) + k
        self.events.append((eng, tuple(wl), fn, tuple(incs)))

    def barrier(self, s_bar):
        tgt = s_bar.n + len(ENGINES)
        for eng in ENGINES:
            drains = tuple((sm.h, sm.n) for sm in self.sems
                           if sm is not s_bar and sm.by_eng.get(eng, 0) > 0
                           and sm.n > 0)

            def fn(e, _t=tgt, _d=drains):
                for h, v in _d:
                    e.wait_ge(h, v)
                e.sem_inc(s_bar.h, 1)
                e.wait_ge(s_bar.h, _t)
            self.events.append((eng, (), fn, ()))
        s_bar.n = tgt
        for eng in ENGINES:
            s_bar.by_eng[eng] = s_bar.by_eng.get(eng, 0) + 1

    def emit(self, blk):
        for eng_name in ENGINES:
            sec = getattr(blk, eng_name)
            evs = [e for e in self.events if e[0] == eng_name]

            def run(engine, _evs=evs):
                for _eng, waits, fn, incs in _evs:
                    for s, v in waits:
                        engine.wait_ge(s.h, v)
                    ins = fn(engine)
                    if incs:
                        for s, k in incs:
                            ins.then_inc(s.h, k)
            sec(run)


def build_program(nc, dbg=()):
    dbg = set(dbg)
    bld = Builder(nc)
    ev, sem, sb = bld.ev, bld.sem, bld.sb

    xbd = nc.dram_tensor("xb", [T, E], F32, kind="ExternalInput").ap()
    xqd = nc.dram_tensor("xq", [2048, E], F32, kind="ExternalInput").ap()
    wqd = nc.dram_tensor("wq", [E, E], F32, kind="ExternalInput").ap()
    wqrd = nc.dram_tensor("wqr", [E, 512], F32, kind="ExternalInput").ap()
    wkrd = nc.dram_tensor("wkr", [E, 32], F32, kind="ExternalInput").ap()
    wkvdd = nc.dram_tensor("wkvd", [E, 256], F32, kind="ExternalInput").ap()
    wkud = nc.dram_tensor("wku", [256, E], F32, kind="ExternalInput").ap()
    wvud = nc.dram_tensor("wvu", [256, E], F32, kind="ExternalInput").ap()
    wod = nc.dram_tensor("wo", [E, E], F32, kind="ExternalInput").ap()
    w1d = nc.dram_tensor("w1", [E, FF], F32, kind="ExternalInput").ap()
    w2d = nc.dram_tensor("w2", [FF, E], F32, kind="ExternalInput").ap()
    b1d = nc.dram_tensor("b1", [128, 32], F32, kind="ExternalInput").ap()
    b2bcd = nc.dram_tensor("b2bc", [128, E], F32, kind="ExternalInput").ap()
    lobod = nc.dram_tensor("lobo_bc", [128, H], F32, kind="ExternalInput").ap()
    cqtd = nc.dram_tensor("cqt", [128, 256], F32, kind="ExternalInput").ap()
    sqtd = nc.dram_tensor("sqt", [128, 256], F32, kind="ExternalInput").ap()
    cktd = nc.dram_tensor("ckt", [128, 512], F32, kind="ExternalInput").ap()
    sktd = nc.dram_tensor("skt", [128, 512], F32, kind="ExternalInput").ap()
    idnd = nc.dram_tensor("idn", [128, 128], F32, kind="ExternalInput").ap()
    m12d = nc.dram_tensor("m12", [128, 256], F32, kind="ExternalInput").ap()
    outd = nc.dram_tensor("out", [2048, E], F32, kind="ExternalOutput").ap()
    w1bf = nc.dram_tensor("w1bf", [E, FF], BF16, kind="Internal").ap()
    w2bf = nc.dram_tensor("w2bf", [FF, E], BF16, kind="Internal").ap()

    DBG = {}
    def dbg_t(name, shape, dt=BF16):
        if name in dbg:
            DBG[name] = nc.dram_tensor("dbg_" + name, shape, dt,
                                       kind="ExternalOutput").ap()
        return DBG.get(name)

    d_xnq = dbg_t("xnq", [128, 8 * 2048])
    d_lat = dbg_t("lat", [128, 2 * T])
    d_kr = dbg_t("kr", [32, T])
    d_qr = dbg_t("qr", [128, 4 * 2048])
    d_qc = dbg_t("qc", [128, 8 * 2048])
    d_tp = dbg_t("tp", [128, 32 * 512], F32)
    d_hq = dbg_t("hq", [128, 32 * 512])
    d_kasm = dbg_t("kasm", [128, T])
    d_vtok = dbg_t("vtok", [128, 32 * 64])
    d_E = dbg_t("E", [128, 2 * T])
    d_pt = dbg_t("pt", [128, 32 * 256])
    d_sc0 = dbg_t("sc0", [128, 512], F32)
    d_e2 = dbg_t("e2", [128, 2 * 512])
    d_qh = dbg_t("qh", [128, 2048])
    d_ka = dbg_t("ka", [128, T])
    d_den = dbg_t("den", [128, 16], F32)
    d_y = dbg_t("y", [128, 8 * 2048])

    s_cst = sem("cst", dma=True)
    s_wat = sem("wat", dma=True)
    s_wf = sem("wf", dma=True)
    s_wf2 = sem("wf2", dma=True)
    s_xin = sem("xin", dma=True)
    s_tr = sem("tr", dma=True)
    s_xout = sem("xout", dma=True)
    s_w1l = sem("w1l", dma=True)
    s_w2l = sem("w2l", dma=True)
    s_vtr = sem("vtr", dma=True)
    s_etr = sem("etr", dma=True)
    s_oadd = sem("oadd", dma=True)
    s_wol = sem("wol", dma=True)
    s_rtr = sem("rtr", dma=True)
    s_dbg = sem("dbgs", dma=True)
    s_sq = sem("sq"); s_rcp = sem("rcp"); s_ln3 = sem("ln3")
    s_pmm = sem("pmm"); s_pev = sem("pev"); s_rope = sem("rope")
    s_bar = sem("bar")
    s_hmm = sem("hmm"); s_hev = sem("hev"); s_fv = sem("fv"); s_fg = sem("fg")
    s_fgp = sem("fgp"); s_fhq = sem("fhq"); s_omm = sem("omm"); s_oev = sem("oev")
    s_kmm = sem("kmm"); s_kev = sem("kev"); s_qcp = sem("qcp")
    s_smm = sem("smm"); s_sdg = sem("sdg"); s_sex = sem("sex"); s_red = sem("red")
    s_gmx = sem("gmx")
    s_etp = [sem("etrA", dma=True), sem("etrB", dma=True)]
    s_pv = sem("pv"); s_pve = sem("pve")
    s_wmm = sem("wmm"); s_wev = sem("wev")
    s_vh = sem("vh")
    vh_n = [0]
    s_gh = sem("gh")
    gh_n = [0]

    def ghz(e, ins):
        gh_n[0] += 1
        ins.then_inc(s_gh.h, 1)
        e.wait_ge(s_gh.h, gh_n[0])

    def vhz(e, ins):
        vh_n[0] += 1
        ins.then_inc(s_vh.h, 1)
        e.wait_ge(s_vh.h, vh_n[0])

    latent = sb("latent", [128, 2, T], BF16)
    qr_rp = sb("qr_rp", [128, 4, 2048], BF16)
    kr_rp = sb("kr_rp", [32, T], BF16)
    qc_all = sb("qc_all", [128, 8, 2048], BF16)
    c_tile = sb("c_tile", [128, H], F32)
    m12 = sb("m12sb", [128, 256], F32)
    u_t = sb("u_t", [128, 64], F32)
    wku_sb = sb("wku_sb", [128, 2, E], BF16)
    wvu_sb = sb("wvu_sb", [128, 2, E], BF16)

    ps_cm = nc.psum_tensor("ps", [128, 4096], F32)
    PS = ps_cm.__enter__()
    bld.ctx.append(ps_cm)

    def groups_of(km):
        gs, s = [], 0
        while s < km:
            gs.append((s, min(1536, km - s)))
            s += 1536
        return gs

    for dst, src in ((m12[:], m12d), (c_tile[:], lobod)):
        ev("sync", (lambda e, d=dst, s=src: e.dma_start(out=d, in_=s)),
           incs=[(s_cst, 16)])

    xnq_cm = nc.sbuf_tensor("xnTq", [128, 8, 2048], BF16)
    XNQ = xnq_cm.__enter__()
    ab_cms = [
        nc.sbuf_tensor("xtile", [128, 2, E], F32),
        nc.sbuf_tensor("xnt", [128, 2, E], BF16),
        nc.sbuf_tensor("stat", [128, 16], F32),
        nc.sbuf_tensor("sq_scr", [128, 1024], F32),
        nc.sbuf_tensor("slab", [128, 2, 8, 512], BF16),
        nc.sbuf_tensor("cqt_sb", [128, 16, 16], F32),
        nc.sbuf_tensor("sqt_sb", [128, 16, 16], F32),
        nc.sbuf_tensor("ckt_sb", [128, 32, 16], F32),
        nc.sbuf_tensor("skt_sb", [128, 32, 16], F32),
        nc.sbuf_tensor("idn_sb", [128, 128], F32),
        nc.sbuf_tensor("qrt", [128, 512], F32),
        nc.sbuf_tensor("qrb", [128, 512], BF16),
        nc.sbuf_tensor("krt", [128, 128], F32),
        nc.sbuf_tensor("krb", [128, 128], F32),
        nc.sbuf_tensor("wkvd_sb", [128, 8, 256], BF16),
        nc.sbuf_tensor("wkr_sb", [128, 8, 32], BF16),
        nc.sbuf_tensor("wqr_sb", [128, 8, 512], BF16),
        nc.sbuf_tensor("wq_sb", [128, 8, 1024], BF16),
        nc.sbuf_tensor("prj", [128, 2, 512], F32),
    ]
    (xtile, xnt, stat, sq_scr, SLAB, CQT, SQT, CKT, SKT, IDN, QRT, QRB,
     KRT, KRB, WKVD, WKR, WQR, WQ, PRJ) = [cm.__enter__() for cm in ab_cms]


    for dst, src, n3 in ((CQT, cqtd, 16), (SQT, sqtd, 16), (CKT, cktd, 32),
                         (SKT, sktd, 32), (IDN, idnd, 0)):
        def fld(e, d=dst, s=src, n3=n3):
            o = d[:] if n3 == 0 else d[:, 0:n3, :].rearrange("p a b -> p (a b)")
            return e.dma_start(out=o, in_=s)
        ev("sync", fld, incs=[(s_cst, 16)])
    ev("scalar", lambda e: e.activation(c_tile[:], c_tile[:], AF.Exp),
       waits=[s_cst], incs=[(s_sq, 1)])

    for dc in range(8):
        for dst, src in ((WKVD, wkvdd), (WKR, wkrd), (WQR, wqrd), (WQ, wqd)):
            ev("gpsimd", (lambda e, d=dst, s=src, c=dc: e.dma_start(
                out=d[:, c, :], in_=s[c * 128:(c + 1) * 128, :])),
               incs=[(s_wat, 16)])
    for lc in range(2):
        for dst, src in ((wku_sb, wkud), (wvu_sb, wvud)):
            ev("gpsimd", (lambda e, d=dst, s=src, c=lc: e.dma_start(
                out=d[:, c, :], in_=s[c * 128:(c + 1) * 128, :])),
               incs=[(s_wat, 16)])
    def emit_wf_casts():
        for i in range(8):
            ev("gpsimd", (lambda e, i=i: e.dma_start(
                out=w1bf[i * 128:(i + 1) * 128, :],
                in_=w1d[i * 128:(i + 1) * 128, :])), incs=[(s_wf, 16)])
        for i in range(8):
            ev("gpsimd", (lambda e, i=i: e.dma_start(
                out=w2bf[i * 512:(i + 1) * 512, :],
                in_=w2d[i * 512:(i + 1) * 512, :])), incs=[(s_wf2, 16)])


    ln_ctr = [0]

    def ln_tile(src_rows, tr_out, out_write=None, tr_waits=()):
        i = ln_ctr[0]
        ln_ctr[0] += 1
        pb = i % 2
        p = 8 * pb
        lw = [s_ln3, s_xout]
        if i >= 1:
            lw.append((s_sq, 2 * i))
        ev("sync", (lambda e: e.dma_start(out=xtile[:, pb, :], in_=src_rows)),
           waits=lw, incs=[(s_xin, 16)])
        ev("scalar", (lambda e: e.activation(
            sq_scr[:], xtile[:, pb, :], AF.Square,
            accum_out=stat[:, p + 1:p + 2])),
           waits=[s_xin], incs=[(s_sq, 1)])

        def fv(e):
            xt = xtile[:, pb, :]
            vhz(e, e.reduce_sum(stat[:, p:p + 1], xt, axis=AX.X))
            e.tensor_scalar_mul(stat[:, p + 2:p + 3], stat[:, p:p + 1], 1.0 / 1024.0)
            vhz(e, e.tensor_scalar_mul(stat[:, p + 3:p + 4],
                                       stat[:, p + 1:p + 2], 1.0 / 1024.0))
            vhz(e, e.tensor_tensor(stat[:, p + 4:p + 5], stat[:, p + 2:p + 3],
                                   stat[:, p + 2:p + 3], ALU.mult))
            vhz(e, e.tensor_tensor(stat[:, p + 3:p + 4], stat[:, p + 3:p + 4],
                                   stat[:, p + 4:p + 5], ALU.subtract))
            vhz(e, e.tensor_scalar_add(stat[:, p + 3:p + 4],
                                       stat[:, p + 3:p + 4], LN_EPS))
            return e.reciprocal(stat[:, p + 5:p + 6], stat[:, p + 3:p + 4])
        ev("vector", fv, waits=[s_xin, s_sq], incs=[(s_rcp, 1)])
        ev("scalar", (lambda e: e.activation(
            stat[:, p + 6:p + 7], stat[:, p + 5:p + 6], AF.Sqrt)),
           waits=[s_rcp], incs=[(s_sq, 1)])
        ev("vector", (lambda e: e.tensor_scalar(
            xnt[:, pb, :], xtile[:, pb, :], stat[:, p + 2:p + 3],
            stat[:, p + 6:p + 7], ALU.subtract, ALU.mult)),
           waits=[s_sq, s_tr], incs=[(s_ln3, 1)])
        ev("sync", (lambda e: e.dma_start(
            out=tr_out,
            in_=xnt[:, pb, :].rearrange("p (k c) -> p k c", c=128),
            transpose=True)),
           waits=[s_ln3] + list(tr_waits), incs=[(s_tr, 16)])
        if out_write is not None:
            ev("sync", (lambda e: e.dma_start(out=out_write, in_=xtile[:, pb, :])),
               waits=[s_ln3], incs=[(s_xout, 16)])

    jmm = [0]

    def proj_bank():
        bk = PS[:, (jmm[0] % 4) * 512:(jmm[0] % 4) * 512 + 512]
        need_wait = jmm[0] >= 4
        jmm[0] += 1
        return bk, need_wait

    for sg in range(8):
        sp = sg % 2
        for j in range(4):
            t = 4 * sg + j
            ln_tile(xbd[t * 128:(t + 1) * 128, :],
                    SLAB[:, sp, 0:8, j * 128:(j + 1) * 128], tr_waits=(s_pmm,))
        for a in (0, 1):
            bk, nw = proj_bank()

            def fmm(e, a=a, sp=sp, bk=bk):
                ins = None
                for dc in range(8):
                    ins = e.matmul(bk, WKVD[:, dc, a * 128:(a + 1) * 128],
                                   SLAB[:, sp, dc, :],
                                   start=(dc == 0), stop=(dc == 7))
                return ins
            w = [s_tr, s_wat] + ([s_pev] if nw else [])
            ev("tensor", fmm, waits=w, incs=[(s_pmm, 1)])
            ev("scalar", (lambda e, a=a, sg=sg, bk=bk: e.activation(
                latent[:, a, sg * 512:(sg + 1) * 512], bk, AF.Copy)),
               waits=[s_pmm], incs=[(s_pev, 1)])
        bk, nw = proj_bank()

        def fkrp(e, sp=sp, bk=bk):
            ins = None
            for j in range(4):
                for dc in range(8):
                    ins = e.matmul(bk[:, j * 32:(j + 1) * 32],
                                   SLAB[:, sp, dc, j * 128:(j + 1) * 128],
                                   WKR[:, dc, :],
                                   start=(dc == 0), stop=(dc == 7))
            return ins
        w = [s_tr, s_wat] + ([s_pev] if nw else [])
        ev("tensor", fkrp, waits=w, incs=[(s_pmm, 1)])
        ev("scalar", (lambda e, bk=bk: e.activation(KRT[:], bk[:, 0:128],
                                                    AF.Copy)),
           waits=[s_pmm, s_rope], incs=[(s_pev, 1)])

        def frope_k(e, sg=sg):
            v4 = lambda ap: ap.rearrange("p (t two j) -> p t two j", two=2, j=16)
            k1 = v4(KRT[:])[:, :, 0:1, :]
            k2 = v4(KRT[:])[:, :, 1:2, :]
            c = CKT[:, 4 * sg:4 * sg + 4, :].unsqueeze(2)
            sn = SKT[:, 4 * sg:4 * sg + 4, :].unsqueeze(2)
            t1 = v4(sq_scr[:, 0:128])[:, :, 0:1, :]
            t2 = v4(sq_scr[:, 128:256])[:, :, 0:1, :]
            r0 = v4(KRB[:])[:, :, 0:1, :]
            r1 = v4(KRB[:])[:, :, 1:2, :]
            vhz(e, e.tensor_tensor(t1, k1, c, ALU.mult))
            vhz(e, e.tensor_tensor(t2, k2, sn, ALU.mult))
            vhz(e, e.tensor_tensor(r0, t1, t2, ALU.subtract))
            vhz(e, e.tensor_tensor(t1, k2, c, ALU.mult))
            vhz(e, e.tensor_tensor(t2, k1, sn, ALU.mult))
            return e.tensor_tensor(r1, t1, t2, ALU.add)
        ev("vector", frope_k, waits=[s_pev, s_cst, s_rtr], incs=[(s_rope, 1)])
        bk2, nw2 = proj_bank()

        def fkrt(e, bk2=bk2):
            ins = None
            for j in range(4):
                ins = e.transpose(bk2[0:32, j * 128:(j + 1) * 128],
                                  KRB[:, j * 32:(j + 1) * 32], IDN[:])
            return ins
        ev("tensor", fkrt, waits=[s_rope, s_cst] + ([s_pev] if nw2 else []),
           incs=[(s_pmm, 1)])
        ev("scalar", (lambda e, sg=sg, bk2=bk2: e.activation(
            kr_rp[0:32, sg * 512:(sg + 1) * 512], bk2[0:32, :], AF.Copy)),
           waits=[s_pmm], incs=[(s_pev, 1)])

    for oi in range(16):
        ln_tile(xqd[oi * 128:(oi + 1) * 128, :],
                XNQ[:, 0:8, oi * 128:(oi + 1) * 128],
                out_write=outd[oi * 128:(oi + 1) * 128, :])

    if d_xnq is not None:
        for dc in range(8):
            ev("gpsimd", (lambda e, dc=dc: e.dma_start(
                out=d_xnq[:, dc * 2048:(dc + 1) * 2048], in_=XNQ[:, dc, :])),
               waits=[s_tr], incs=[(s_dbg, 16)])

    for oi in range(16):
        bk, nw = proj_bank()

        def fqr(e, oi=oi, bk=bk):
            ins = None
            for dc in range(8):
                ins = e.matmul(bk, XNQ[:, dc, oi * 128:(oi + 1) * 128],
                               WQR[:, dc, :], start=(dc == 0), stop=(dc == 7))
            return ins
        w = [s_tr, s_wat] + ([s_pev] if nw else [])
        ev("tensor", fqr, waits=w, incs=[(s_pmm, 1)])
        ev("scalar", (lambda e, bk=bk: e.activation(QRT[:], bk, AF.Copy)),
           waits=[s_pmm, s_rope], incs=[(s_pev, 1)])

        def frope_q(e, oi=oi):
            v4 = lambda ap: ap.rearrange("p (h two j) -> p h two j", two=2, j=16)
            q1 = v4(QRT[:])[:, :, 0:1, :]
            q2 = v4(QRT[:])[:, :, 1:2, :]
            bb = lambda ap: ap.unsqueeze(2).broadcast_to((128, 16, 1, 16))
            c = bb(CQT[:, oi:oi + 1, :])
            sn = bb(SQT[:, oi:oi + 1, :])
            t1 = v4(sq_scr[:, 0:512])[:, :, 0:1, :]
            t2 = v4(sq_scr[:, 512:1024])[:, :, 0:1, :]
            r0 = v4(QRB[:])[:, :, 0:1, :]
            r1 = v4(QRB[:])[:, :, 1:2, :]
            vhz(e, e.tensor_tensor(t1, q1, c, ALU.mult))
            vhz(e, e.tensor_tensor(t2, q2, sn, ALU.mult))
            vhz(e, e.tensor_tensor(r0, t1, t2, ALU.subtract))
            vhz(e, e.tensor_tensor(t1, q2, c, ALU.mult))
            vhz(e, e.tensor_tensor(t2, q1, sn, ALU.mult))
            return e.tensor_tensor(r1, t1, t2, ALU.add)
        ev("vector", frope_q, waits=[s_pev, s_cst, s_rtr], incs=[(s_rope, 1)])
        ev("sync", (lambda e, oi=oi: e.dma_start(
            out=qr_rp[:, 0:4, oi * 128:(oi + 1) * 128],
            in_=QRB[:].rearrange("p (k c) -> p k c", c=128), transpose=True)),
           waits=[s_rope], incs=[(s_rtr, 16)])

    for oc in range(8):
        for g in range(4):
            bk, nw = proj_bank()

            def fq(e, oc=oc, g=g, bk=bk):
                ins = None
                for dc in range(8):
                    ins = e.matmul(bk, WQ[:, dc, oc * 128:(oc + 1) * 128],
                                   XNQ[:, dc, g * 512:(g + 1) * 512],
                                   start=(dc == 0), stop=(dc == 7))
                return ins
            w = [s_tr, s_wat] + ([s_pev] if nw else [])
            ev("tensor", fq, waits=w, incs=[(s_pmm, 1)])
            ev("scalar", (lambda e, oc=oc, g=g, bk=bk: e.activation(
                qc_all[:, oc, g * 512:(g + 1) * 512], bk, AF.Copy)),
               waits=[s_pmm], incs=[(s_pev, 1)])

    if d_lat is not None:
        for a in range(2):
            ev("gpsimd", (lambda e, a=a: e.dma_start(
                out=d_lat[:, a * T:(a + 1) * T], in_=latent[:, a, :])),
               waits=[s_pev], incs=[(s_dbg, 16)])
    if d_kr is not None:
        ev("gpsimd", (lambda e: e.dma_start(out=d_kr, in_=kr_rp[:, :])),
           waits=[s_rope], incs=[(s_dbg, 16)])
    if d_qr is not None:
        for a in range(4):
            ev("gpsimd", (lambda e, a=a: e.dma_start(
                out=d_qr[:, a * 2048:(a + 1) * 2048], in_=qr_rp[:, a, :])),
               waits=[s_rtr], incs=[(s_dbg, 16)])
    if d_qc is not None:
        for a in range(8):
            ev("gpsimd", (lambda e, a=a: e.dma_start(
                out=d_qc[:, a * 2048:(a + 1) * 2048], in_=qc_all[:, a, :])),
               waits=[s_pev], incs=[(s_dbg, 16)])

    ev("sync", lambda e: e.nop(), waits=[s_xout, s_tr, s_rtr, s_cst, s_xin])
    ev("gpsimd", lambda e: e.nop(), waits=[s_dbg, s_wat])
    bld.barrier(s_bar)
    for cm in reversed(ab_cms):
        cm.__exit__(None, None, None)

    emit_wf_casts()

    c_cms = [
        nc.sbuf_tensor("HQ", [128, 32, 512], BF16),
        nc.sbuf_tensor("FT", [128, 16, 512], F32),
        nc.sbuf_tensor("W1C", [128, 2, 8, 128], BF16),
        nc.sbuf_tensor("W2C", [128, 2, 1024], BF16),
        nc.sbuf_tensor("FT2", [128, 4, 1024], F32),
        nc.sbuf_tensor("b2t", [128, E], F32),
        nc.sbuf_tensor("b1t", [128, 32], F32),
        nc.sbuf_tensor("cgel", [128, 2], F32),
    ]
    (HQ, FT, W1C, W2C, FT2, b2t, b1t,
     cgel) = [cm.__enter__() for cm in c_cms]

    ev("sync", (lambda e: e.dma_start(out=b2t[:], in_=b2bcd)),
       incs=[(s_cst, 16)])
    ev("sync", (lambda e: e.dma_start(out=b1t[:, 0:32], in_=b1d)),
       incs=[(s_cst, 16)])
    ev("vector", lambda e: e.tensor_scalar(b2t[:], b2t[:], 256.0,
                                           16384.0, ALU.mult, ALU.add),
       waits=[s_cst], incs=[(s_rcp, 1)])
    ev("vector", lambda e: e.tensor_scalar(b1t[:, 0:32], b1t[:, 0:32], 256.0,
                                           16384.0, ALU.mult, ALU.add),
       waits=[s_cst], incs=[(s_rcp, 1)])
    ev("vector", lambda e: e.memset(cgel[:, 0:1], -64.0),
       incs=[(s_rcp, 1)])
    ev("vector", lambda e: e.memset(cgel[:, 1:2], -63.75), incs=[(s_rcp, 1)])

    def ft(slot, p):
        return FT[:, 8 * p + slot, :]

    def w1_load(fc):
        def f(e, fc=fc):
            src = w1bf[:, fc * 128:(fc + 1) * 128].rearrange(
                "(dc p) f -> p dc f", p=128)
            return e.dma_start(out=W1C[:, fc % 2, :, :], in_=src)
        ev("sync", f, waits=[s_wf, s_hmm], incs=[(s_w1l, 16)])

    hev_hist, fv_hist, fgp_hist, fhq_hist = {}, {}, {}, {}
    ommg, w2ln = {}, {}
    pend_v = [None]
    pend_s = [None]
    from collections import deque
    pend_o = deque()

    def drain_o(nmax):
        c = 0
        while pend_o and c < nmax:
            pend_o.popleft()()
            c += 1

    def w2_load_g(odx, fcl):
        def f(e, odx=odx, fcl=fcl):
            return e.dma_start(out=W2C[:, odx % 2, :],
                               in_=w2bf[fcl * 128:(fcl + 1) * 128, :])
        ev("sync", f, waits=[s_wf2, (s_w2l, s_w2l.n)] +
           ([(s_omm, ommg[odx - 2])] if odx >= 2 else []),
           incs=[(s_w2l, 16)])
        w2ln[odx] = s_w2l.n

    def emit_foq(tg, P):
        for tcl in range(2):
            tc = 2 * P + tcl

            def foq(e, tcl=tcl, tc=tc):
                A = FT2[:, 0, :]
                Bf = FT2[:, 1, :]
                OUT = FT2[:, 2 + tc % 2, :]
                psl = PS[:, 2048 + tcl * 1024:2048 + (tcl + 1) * 1024]
                vhz(e, e.scalar_tensor_tensor(A, psl, 256.0, b2t[:],
                                              ALU.mult, ALU.add))
                vhz(e, e.tensor_scalar(Bf, A, MAGIC, -MAGIC,
                                       ALU.add, ALU.add))
                vhz(e, e.scalar_tensor_tensor(A, A, -0.5, Bf,
                                              ALU.add, ALU.max))
                return e.tensor_scalar(OUT, A, 1.0 / 256.0, -64.0,
                                       ALU.mult, ALU.add)
            ev("vector", foq, waits=[s_omm, s_oadd, s_rcp],
               incs=[(s_oev, 1)])
            oi = tg * 4 + tc
            ev("gpsimd", (lambda e, oi=oi, tc=tc: e.dma_start(
                out=outd[oi * 128:(oi + 1) * 128, :],
                in_=FT2[:, 2 + tc % 2, :], accum_op=ALU.add)),
               waits=[s_oev, s_xout], incs=[(s_oadd, 16)])

    def mk_fo(tg, P, fc):
        odx = tg * 64 + P * 32 + fc
        jb = tg * 32 + fc

        def go():
            if fc == 0:
                w2_load_g(odx, fc)
                w2_load_g(odx + 1, fc + 1)

            def fo(e, tg=tg, P=P, fc=fc, odx=odx):
                ins = None
                for tcl in range(2):
                    tc = 2 * P + tcl
                    for oh in range(2):
                        ins = e.matmul(
                            PS[:, 2048 + tcl * 1024 + oh * 512:
                               2048 + tcl * 1024 + oh * 512 + 512],
                            HQ[:, fc, tc * 128:(tc + 1) * 128],
                            W2C[:, odx % 2, oh * 512:(oh + 1) * 512],
                            start=(fc == 0), stop=(fc == 31))
                return ins
            ev("tensor", fo,
               waits=[(s_fhq, fhq_hist[jb]), s_oev, (s_w2l, w2ln[odx])],
               incs=[(s_omm, 1)])
            ommg[odx] = s_omm.n
            if fc + 2 < 32:
                w2_load_g(odx + 2, fc + 2)
            if fc == 31:
                emit_foq(tg, P)
        return go

    for tg in range(4):
        w1_load(0)
        w1_load(1)
        for fc in range(32):
            p = fc % 2
            jb = tg * 32 + fc
            bkh = PS[:, (jb % 4) * 512:(jb % 4) * 512 + 512]

            def fh(e, tg=tg, fc=fc, bkh=bkh):
                ins = None
                for dc in range(8):
                    ins = e.matmul(bkh, W1C[:, fc % 2, dc, :],
                                   XNQ[:, dc, tg * 512:(tg + 1) * 512],
                                   start=(dc == 0), stop=(dc == 7))
                return ins
            whl = [s_tr]
            if jb >= 4:
                whl.append((s_hev, hev_hist[jb - 4]))
            if fc < 2:
                whl.append((s_w1l, s_w1l.n))
            else:
                whl.append(s_w1l)
            ev("tensor", fh, waits=whl, incs=[(s_hmm, 1)])
            drain_o(2)
            if fc + 2 < 32:
                w1_load(fc + 2)

            ev("scalar", (lambda e, fc=fc, p=p, bkh=bkh: e.activation(
                ft(0, p), bkh, AF.Identity, bias=b1t[:, fc:fc + 1],
                scale=256.0)), waits=[s_hmm, s_rcp] +
               ([(s_fv, fv_hist[jb - 2])] if jb >= 2 else []),
               incs=[(s_hev, 1)])
            hev_hist[jb] = s_hev.n
            if pend_s[0] is not None:
                pend_s[0]()
                pend_s[0] = None

            if tg == 0 and d_tp is not None:
                ev("gpsimd", (lambda e, fc=fc, p=p: e.dma_start(
                    out=d_tp[:, fc * 512:(fc + 1) * 512], in_=ft(0, p))),
                   waits=[s_hev], incs=[(s_dbg, 16)])

            def fvc1(e, p=p):
                vhz(e, e.tensor_scalar(ft(1, p), ft(0, p), MAGIC, -MAGIC,
                                       ALU.add, ALU.add))
                vhz(e, e.scalar_tensor_tensor(ft(2, p), ft(0, p), -0.5,
                                              ft(1, p), ALU.add, ALU.max))
                vhz(e, e.tensor_scalar(ft(3, p), ft(2, p), 1.0 / 64.0, -0.5,
                                       ALU.mult, ALU.add))
                vhz(e, e.tensor_scalar(ft(1, p), ft(3, p), MAGIC, -MAGIC,
                                       ALU.add, ALU.add))
                vhz(e, e.scalar_tensor_tensor(ft(4, p), ft(3, p), -0.5,
                                              ft(1, p), ALU.add, ALU.max))
                return e.scalar_tensor_tensor(ft(5, p), ft(4, p), -64.0,
                                              ft(2, p), ALU.mult, ALU.add)
            ev("vector", fvc1, waits=[s_hev] +
               ([(s_fgp, fgp_hist[jb - 2])] if jb >= 2 else []) +
               ([s_dbg] if (tg == 0 and d_tp is not None) else []),
               incs=[(s_fv, 1)])
            fv_hist[jb] = s_fv.n
            if pend_v[0] is not None:
                pend_v[0]()
                pend_v[0] = None

            def q_gel(p=p, jb=jb):
                def fgel(e, p=p):
                    e.activation(ft(6, p), ft(4, p), AF.Gelu,
                                 bias=cgel[:, 0:1], scale=0.25)
                    return e.activation(ft(7, p), ft(4, p), AF.Gelu,
                                        bias=cgel[:, 1:2], scale=0.25)
                ev("scalar", fgel, waits=[(s_fv, fv_hist[jb]), s_rcp],
                   incs=[(s_fg, 1)])

                def fgps(e, p=p):
                    ghz(e, e.tensor_tensor(ft(3, p), ft(7, p), ft(6, p),
                                           ALU.subtract))
                    return e.tensor_tensor(ft(1, p), ft(3, p), ft(5, p),
                                           ALU.mult)
                ev("gpsimd", fgps, waits=[s_fg, (s_fv, fv_hist[jb])] +
                   ([(s_fhq, fhq_hist[jb - 2])] if jb >= 2 else []),
                   incs=[(s_fgp, 1)])
                fgp_hist[jb] = s_fgp.n
            pend_s[0] = q_gel

            def q_vc2(fc=fc, p=p, jb=jb, tg=tg):
                def fvc2(e, fc=fc, p=p):
                    vhz(e, e.tensor_scalar(ft(3, p), ft(1, p), 4.0, 16384.0,
                                           ALU.mult, ALU.add))
                    vhz(e, e.scalar_tensor_tensor(ft(2, p), ft(6, p), 256.0,
                                                  ft(3, p), ALU.mult,
                                                  ALU.add))
                    vhz(e, e.tensor_scalar(ft(3, p), ft(2, p), MAGIC, -MAGIC,
                                           ALU.add, ALU.add))
                    vhz(e, e.scalar_tensor_tensor(ft(4, p), ft(2, p), -0.5,
                                                  ft(3, p), ALU.add, ALU.max))
                    return e.tensor_scalar(HQ[:, fc, :], ft(4, p), 1.0 / 256.0,
                                           -64.0, ALU.mult, ALU.add)
                ev("vector", fvc2, waits=[(s_fgp, fgp_hist[jb])] +
                   ([(s_omm, ommg[(tg - 1) * 64 + 32 + fc])] if tg >= 1
                    else []),
                   incs=[(s_fhq, 1)])
                fhq_hist[jb] = s_fhq.n
                pend_o.append(mk_fo(tg, 0, fc))
                if fc == 31:
                    for fc2 in range(32):
                        pend_o.append(mk_fo(tg, 1, fc2))
            pend_v[0] = q_vc2

        if tg == 0 and d_hq is not None:
            for fc in range(32):
                ev("gpsimd", (lambda e, fc=fc: e.dma_start(
                    out=d_hq[:, fc * 512:(fc + 1) * 512], in_=HQ[:, fc, :])),
                   waits=[s_fhq], incs=[(s_dbg, 16)])

        if pend_s[0] is not None:
            pend_s[0]()
            pend_s[0] = None
        if pend_v[0] is not None:
            pend_v[0]()
            pend_v[0] = None

    while pend_o:
        pend_o.popleft()()

    ev("sync", lambda e: e.nop(), waits=[s_w1l, s_w2l])
    ev("gpsimd", lambda e: e.nop(), waits=[s_oadd, s_dbg, s_wf, s_wf2])
    bld.barrier(s_bar)
    for cm in reversed(c_cms):
        cm.__exit__(None, None, None)
    xnq_cm.__exit__(None, None, None)

    y_cm = nc.sbuf_tensor("y_sb", [128, 8, 2048], BF16)
    y_sb = y_cm.__enter__()
    d_cms = [
        nc.sbuf_tensor("kasm", [128, 2, T], BF16),
        nc.sbuf_tensor("vfeat", [64, T], BF16),
        nc.sbuf_tensor("vtok", [128, 2, 32, 64], BF16),
        nc.sbuf_tensor("qh", [128, 2, 2048], BF16),
        nc.sbuf_tensor("Ebuf", [128, 3, T], BF16),
        nc.sbuf_tensor("PT", [128, 60, 256], BF16),
    ]
    KA, VF, VT, QH, EB, PT = [cm.__enter__() for cm in d_cms]


    gctr = [0]
    kev_i = [0]
    pend_pv = [None]
    smm_hd, vtr_hd, pv_hd = {}, {}, {}
    sex_g, sdg_g = {}, {}
    etr_pr, pve_pr, red_pr, pv_pr = {}, {}, {}, {}

    for h in range(H):
        hp = h % 2
        for wsb, is_k in ((wku_sb, True), (wvu_sb, False)):
            for g in range(8):
                i = kev_i[0]
                slot = i % 2

                def fkc(e, h=h, g=g, slot=slot, wsb=wsb):
                    ins = None
                    for lc in range(2):
                        ins = e.matmul(PS[slot * 64:slot * 64 + 64, 3072:3584],
                                       wsb[:, lc, h * 64:(h + 1) * 64],
                                       latent[:, lc, g * 512:(g + 1) * 512],
                                       start=(lc == 0), stop=(lc == 1))
                    return ins
                ev("tensor", fkc, waits=[s_wat] +
                   ([(s_kev, i - 1)] if i >= 2 else []), incs=[(s_kmm, 1)])
                if is_k:
                    dst = KA[0:64, hp, g * 512:(g + 1) * 512]
                    lagw = [(s_smm, smm_hd[h - 2])] if h >= 2 else []
                else:
                    dst = VF[:, g * 512:(g + 1) * 512]
                    lagw = [(s_vtr, vtr_hd[h - 1])] if h >= 1 else []
                ev("scalar", (lambda e, dst=dst, slot=slot: e.activation(
                    dst, PS[slot * 64:slot * 64 + 64, 3072:3584], AF.Copy)),
                   waits=[s_kmm] + lagw, incs=[(s_kev, 1)])
                kev_i[0] += 1
        ev("sync", (lambda e, hp=hp: e.dma_start(
            out=VT[:, hp, :, :],
            in_=VF[:, :].rearrange("p (k c) -> p k c", c=128),
            transpose=True)),
           waits=[s_kev, (s_vtr, s_vtr.n)] +
           ([(s_pv, pv_hd[h - 2])] if h >= 2 else []),
           incs=[(s_vtr, 16)])
        vtr_hd[h] = s_vtr.n
        if h < 2:
            ev("vector", (lambda e, hp=hp: e.tensor_copy(
                KA[64:96, hp, :], kr_rp[:, :])),
               waits=[s_rope], incs=[(s_qcp, 1)])

        def fqa(e, h=h, hp=hp):
            e.tensor_copy(QH[0:64, hp, :],
                          qc_all[(h % 2) * 64:(h % 2) * 64 + 64, h // 2, :])
            return e.tensor_copy(QH[64:96, hp, :],
                                 qr_rp[(h % 4) * 32:(h % 4) * 32 + 32,
                                       h // 4, :])
        ev("vector", fqa, waits=[s_qcp] +
           ([(s_smm, smm_hd[h - 2])] if h >= 2 else []),
           incs=[(s_qcp, 1)])
        if h == 0 and d_qh is not None:
            ev("gpsimd", (lambda e: e.dma_start(out=d_qh, in_=QH[:, 0, :])),
               waits=[s_qcp], incs=[(s_dbg, 16)])
        if h == 0 and d_ka is not None:
            ev("gpsimd", (lambda e: e.dma_start(out=d_ka, in_=KA[:, 0, :])),
               waits=[s_qcp, s_kev], incs=[(s_dbg, 16)])

        for pr in range(8):
            k = h * 8 + pr
            oiA, oiB = 2 * pr, 2 * pr + 1
            for rr, oi in ((0, oiA), (1, oiB)):
                km = KMU[oi]
                grps = groups_of(km)
                slotE = (2 * k + rr) % 3
                ub = (pr % 2) * 32 + rr * 16
                mb = [(km - 256, 0), (km - 128, 128)]
                for gi, (gs, gl) in enumerate(grps):
                    g = gctr[0]
                    tb = 3 * (g % 2)
                    gctr[0] += 1

                    def fsc(e, hp=hp, oi=oi, gs=gs, gl=gl, tb=tb):
                        ins = None
                        m = 0
                        while m * 512 < gl:
                            wd = min(512, gl - m * 512)
                            ins = e.matmul(
                                PS[:, (tb + m) * 512:(tb + m) * 512 + wd],
                                QH[0:96, hp, oi * 128:(oi + 1) * 128],
                                KA[0:96, hp, gs + m * 512:gs + m * 512 + wd],
                                start=True, stop=True)
                            m += 1
                        return ins
                    ev("tensor", fsc, waits=[s_qcp, s_kev] +
                       ([(s_sex, sex_g[g - 2]), (s_sdg, sdg_g[g - 2])]
                        if g >= 2 else []) +
                       ([s_dbg] if (h == 0 and d_qh is not None) else []),
                       incs=[(s_smm, 1)])
                    nmask = 0
                    for pos, mcol in mb:
                        if gs <= pos < gs + gl:
                            off = pos - gs

                            def fmask(e, tb=tb, off=off, mcol=mcol):
                                return e.tensor_tensor(
                                    PS[:, tb * 512 + off:tb * 512 + off + 128],
                                    PS[:, tb * 512 + off:tb * 512 + off + 128],
                                    m12[:, mcol:mcol + 128], ALU.add)
                            ev("vector", fmask, waits=[s_smm] +
                               ([s_dbg] if (h == 0 and pr == 0 and rr == 0
                                            and d_sc0 is not None) else []),
                               incs=[(s_sdg, 1)])
                            nmask += 1

                    def fex(e, slotE=slotE, gs=gs, gl=gl, tb=tb, ub=ub,
                            gi=gi):
                        return e.activation(
                            EB[:, slotE, gs:gs + gl],
                            PS[:, tb * 512:tb * 512 + gl],
                            AF.Exp, scale=SCALE96,
                            accum_out=u_t[:, ub + gi:ub + gi + 1])
                    klag = k - 2 if rr == 0 else k - 1
                    ev("scalar", fex, waits=[s_smm, s_sdg] +
                       ([etr_pr[klag]] if klag >= 0 else []) +
                       ([(s_red, red_pr[k - 2])] if k >= 2 else []),
                       incs=[(s_sex, 1)])
                    sex_g[g] = s_sex.n
                    sdg_g[g] = s_sdg.n
                    ev("vector", (lambda e, slotE=slotE, gs=gs, gl=gl,
                                  ub=ub, gi=gi: e.reduce_max(
                                      u_t[:, ub + 3 + gi:ub + 4 + gi],
                                      EB[:, slotE, gs:gs + gl], axis=AX.X)),
                       waits=[s_sex], incs=[(s_gmx, 1)])

                if h == 0 and pr == 0 and d_e2 is not None:
                    ev("gpsimd", (lambda e, rr=rr, km=km: e.dma_start(
                        out=d_e2[:, rr * 512:rr * 512 + km],
                        in_=EB[:, rr, 0:km])),
                       waits=[s_sex], incs=[(s_dbg, 16)])

                def fred(e, slotE=slotE, km=km, ng=len(grps), h=h, ub=ub):
                    r8 = ub
                    if ng > 1:
                        vhz(e, e.reduce_sum(u_t[:, r8 + 6:r8 + 7],
                                            u_t[:, r8:r8 + ng], axis=AX.X))
                        vhz(e, e.reduce_max(u_t[:, r8 + 7:r8 + 8],
                                            u_t[:, r8 + 3:r8 + 3 + ng],
                                            axis=AX.X))
                    else:
                        vhz(e, e.tensor_copy(u_t[:, r8 + 6:r8 + 7],
                                             u_t[:, r8:r8 + 1]))
                        vhz(e, e.tensor_copy(u_t[:, r8 + 7:r8 + 8],
                                             u_t[:, r8 + 3:r8 + 4]))
                    vhz(e, e.scalar_tensor_tensor(u_t[:, r8 + 8:r8 + 9],
                                                  u_t[:, r8 + 7:r8 + 8],
                                                  c_tile[:, h:h + 1],
                                                  u_t[:, r8 + 6:r8 + 7],
                                                  ALU.mult, ALU.add))
                    vhz(e, e.reciprocal(u_t[:, r8 + 9:r8 + 10],
                                        u_t[:, r8 + 8:r8 + 9]))
                    return e.tensor_scalar(EB[:, slotE, 0:km],
                                           EB[:, slotE, 0:km],
                                           u_t[:, r8 + 9:r8 + 10], None,
                                           ALU.mult)
                ev("vector", fred, waits=[s_sex, s_gmx] +
                   ([s_dbg] if (h == 0 and pr == 0 and d_e2 is not None)
                    else []), incs=[(s_red, 1)])
                if rr == 0:
                    ev("vector", (lambda e, kmA=KMU[oiA], kmB=KMU[oiB], k=k:
                                  e.memset(EB[:, (2 * k) % 3, kmA:kmB], 0.0)),
                       waits=([etr_pr[k - 2]] if k >= 2 else []),
                       incs=[(s_red, 1)])
                Kb = KMU[oiB] // 128
                pto = 0 if k % 2 == 0 else 28
                sp = s_etp[k % 2]
                ev("sync", (lambda e, rr=rr, K=Kb, k=k, pto=pto: e.dma_start(
                    out=PT[:, pto:pto + K, rr * 128:rr * 128 + 128],
                    in_=EB[:, (2 * k + rr) % 3, 0:K * 128]
                    .rearrange("p (k c) -> p k c", c=128),
                    transpose=True)),
                   waits=[s_red, (sp, sp.n)] +
                   ([(s_pv, pv_pr[k - 2])] if k >= 2 else []),
                   incs=[(sp, 16)])

            if d_den is not None and h == 0 and pr == 0:
                ev("gpsimd", lambda e: e.dma_start(out=d_den, in_=u_t[:]),
                   waits=[s_red], incs=[(s_dbg, 16)])

            kmA, kmB = KMU[oiA], KMU[oiB]
            red_pr[k] = s_red.n
            etr_pr[k] = (s_etp[k % 2], s_etp[k % 2].n)

            if h == 0 and pr == 0 and d_E is not None:
                for rr in range(2):
                    ev("gpsimd", (lambda e, rr=rr: e.dma_start(
                        out=d_E[:, rr * T:rr * T + T], in_=EB[:, rr, :])),
                       waits=[s_red], incs=[(s_dbg, 16)])
            K = kmB // 128
            pto = 0 if k % 2 == 0 else 28

            if pend_pv[0] is not None:
                pend_pv[0]()
                pend_pv[0] = None

            def emit_pv(hp=hp, K=K, pr=pr, pto=pto, k=k, h=h,
                        etr_tgt=etr_pr[k]):
                def fpv(e, hp=hp, K=K, pr=pr, pto=pto):
                    p = (pr % 2) * 64
                    ins = None
                    for kc in range(K):
                        ins = e.matmul(PS[p:p + 64, 3584:3840],
                                       VT[:, hp, kc, :],
                                       PT[:, pto + kc, :], start=(kc == 0),
                                       stop=(kc == K - 1))
                    return ins
                ev("tensor", fpv, waits=[etr_tgt, s_vtr] +
                   ([(s_pve, pve_pr[k - 2])] if k >= 2 else []),
                   incs=[(s_pv, 1)])
                pv_pr[k] = s_pv.n
                ev("scalar", (lambda e, h=h, pr=pr: e.activation(
                    y_sb[(h % 2) * 64:(h % 2) * 64 + 64, h // 2,
                         pr * 256:(pr + 1) * 256],
                    PS[(pr % 2) * 64:(pr % 2) * 64 + 64, 3584:3840], AF.Copy)),
                   waits=[s_pv], incs=[(s_pve, 1)])
                pve_pr[k] = s_pve.n
            pend_pv[0] = emit_pv

        if pend_pv[0] is not None:
            pend_pv[0]()
            pend_pv[0] = None
        smm_hd[h] = s_smm.n
        pv_hd[h] = s_pv.n
        if h == 0:
            if d_kasm is not None:
                ev("gpsimd", (lambda e: e.dma_start(out=d_kasm, in_=KA[:, 0, :])),
                   waits=[s_kev, s_qcp], incs=[(s_dbg, 16)])
            if d_vtok is not None:
                ev("gpsimd", (lambda e: e.dma_start(
                    out=d_vtok,
                    in_=VT[:, 0, :, :].rearrange("p k c -> p (k c)"))),
                   waits=[s_vtr], incs=[(s_dbg, 16)])


    if d_y is not None:
        for a in range(8):
            ev("gpsimd", (lambda e, a=a: e.dma_start(
                out=d_y[:, a * 2048:(a + 1) * 2048], in_=y_sb[:, a, :])),
               waits=[s_pve], incs=[(s_dbg, 16)])

    ev("sync", lambda e: e.nop(),
       waits=[s_etp[0], s_etp[1], s_vtr])
    ev("gpsimd", lambda e: e.nop(), waits=[s_dbg])
    bld.barrier(s_bar)
    for cm in reversed(d_cms):
        cm.__exit__(None, None, None)

    e_cms = [
        nc.sbuf_tensor("WO", [128, 8, 1024], BF16),
        nc.sbuf_tensor("OB", [128, 2, 1024], F32),
    ]
    WOsb, OB = [cm.__enter__() for cm in e_cms]

    for prn in range(8):
        ev("gpsimd", (lambda e, prn=prn: e.dma_start(
            out=WOsb[:, prn, :], in_=wod[prn * 128:(prn + 1) * 128, :])),
           incs=[(s_wol, 16)])

    for oi in range(16):
        pb = oi % 2

        def fwo(e, oi=oi, pb=pb):
            ins = None
            for prn in range(8):
                for oh in range(2):
                    ins = e.matmul(
                        PS[:, pb * 1024 + oh * 512:pb * 1024 + oh * 512 + 512],
                        y_sb[:, prn, oi * 128:(oi + 1) * 128],
                        WOsb[:, prn, oh * 512:(oh + 1) * 512],
                        start=(prn == 0), stop=(prn == 7))
            return ins
        ev("tensor", fwo, waits=[s_wol, s_pve, s_wev], incs=[(s_wmm, 1)])
        ev("vector", (lambda e, pb=pb: e.tensor_copy(
            OB[:, pb, :], PS[:, pb * 1024:(pb + 1) * 1024])),
           waits=[s_wmm, s_oadd], incs=[(s_wev, 1)])
        ev("gpsimd", (lambda e, oi=oi, pb=pb: e.dma_start(
            out=outd[oi * 128:(oi + 1) * 128, :], in_=OB[:, pb, :],
            accum_op=ALU.add)),
           waits=[s_wev], incs=[(s_oadd, 16)])

    ev("gpsimd", lambda e: e.nop(), waits=[s_oadd, s_dbg])
    ev("sync", lambda e: e.nop(), waits=[s_xout])
    for cm in reversed(e_cms):
        cm.__exit__(None, None, None)
    y_cm.__exit__(None, None, None)

    return bld


def kernel(L0_attn_x, L0_attn_Wq, L0_attn_Wqr, L0_attn_Wkr, L0_attn_Wkvd,
           L0_attn_Wku, L0_attn_Wvu, L0_attn_Wo, L0_attn_lobo_log,
           L0_ffn_w1, L0_ffn_b1, L0_ffn_w2, L0_ffn_b2, _dbg=(), _cores=8):
    x = np.ascontiguousarray(np.asarray(L0_attn_x, dtype=np.float32))
    base = {
        "wq": np.ascontiguousarray(np.asarray(L0_attn_Wq, np.float32)),
        "wqr": np.ascontiguousarray(np.asarray(L0_attn_Wqr, np.float32)),
        "wkr": np.ascontiguousarray(np.asarray(L0_attn_Wkr, np.float32)),
        "wkvd": np.ascontiguousarray(np.asarray(L0_attn_Wkvd, np.float32)),
        "wku": np.ascontiguousarray(np.asarray(L0_attn_Wku, np.float32)),
        "wvu": np.ascontiguousarray(np.asarray(L0_attn_Wvu, np.float32)),
        "wo": np.ascontiguousarray(np.asarray(L0_attn_Wo, np.float32)),
        "w1": np.ascontiguousarray(np.asarray(L0_ffn_w1, np.float32)),
        "w2": np.ascontiguousarray(np.asarray(L0_ffn_w2, np.float32)),
        "b1": np.ascontiguousarray(
            np.asarray(L0_ffn_b1, np.float32).reshape(32, 128).T),
        "b2bc": np.ascontiguousarray(
            np.tile(np.asarray(L0_ffn_b2, np.float32)[None, :], (128, 1))),
        "lobo_bc": np.ascontiguousarray(
            np.tile(np.asarray(L0_attn_lobo_log, np.float32)[None, :],
                    (128, 1))),
    }
    halves = [host_tables(0), host_tables(1)]

    nc = bass.Bass("TRN2", target_bir_lowering=False)
    bld = build_program(nc, dbg=_dbg)

    in_maps = []
    idn = np.eye(128, dtype=np.float32)
    for c in range(_cores):
        b, half = c // 2, c % 2
        tiles, cqt, sqt, ckt, skt, m12 = halves[half]
        own_rows = np.concatenate(
            [x[b, t * 128:(t + 1) * 128, :] for t in tiles], axis=0)
        m = dict(base)
        m["xb"] = np.ascontiguousarray(x[b])
        m["xq"] = np.ascontiguousarray(own_rows)
        m["cqt"] = cqt
        m["sqt"] = sqt
        m["ckt"] = ckt
        m["skt"] = skt
        m["idn"] = idn
        m["m12"] = m12
        in_maps.append(m)

    with nc.Block() as blk:
        bld.emit(blk)

    res = run_bass_kernel_spmd(nc, in_maps, core_ids=list(range(_cores)))

    outf = np.empty((B, T, E), np.asarray(res.results[0]["out"]).dtype)
    for c in range(_cores):
        b, half = c // 2, c % 2
        tiles = halves[half][0]
        o = np.asarray(res.results[c]["out"])
        for oi, t in enumerate(tiles):
            outf[b, t * 128:(t + 1) * 128, :] = o[oi * 128:(oi + 1) * 128, :]
    if _dbg:
        kernel._dbg_results = res.results
    return outf


